# version 02
# baseline (speedup 1.0000x reference)
import contextlib
import math
import numpy as np
import ml_dtypes
import concourse.bass as bass
import concourse.mybir as mybir
from concourse.bass_utils import run_bass_kernel_spmd
from concourse.ap import AP

F32 = mybir.dt.float32
BF16 = mybir.dt.bfloat16
AF = mybir.ActivationFunctionType
ALU = mybir.AluOpType
AX = mybir.AxisListType

B = 2
S = 8192
D = 1024
NCORE = 8
TPC = 2048
NCH = D // 128
BF = np.dtype(ml_dtypes.bfloat16)


class Sem:
    def __init__(self, stack, nc, name):
        self.h = stack.enter_context(nc.semaphore(name))
        self.n = 0

    def inc(self, instr, k=1):
        instr.then_inc(self.h, k)
        self.n += k
        return instr

    def wait(self, eng, n=None):
        eng.wait_ge(self.h, self.n if n is None else n)


def dap(handle, off, dims):
    return AP(handle, off, dims)


class LaunchCtx:
    def __init__(self, name):
        self.nc = bass.Bass("TRN2", target_bir_lowering=False)
        self.stack = contextlib.ExitStack()
        self.sems = {}
        self.name = name

    def sem(self, name):
        if name not in self.sems:
            self.sems[name] = Sem(self.stack, self.nc, name)
        return self.sems[name]

    def sbuf(self, name, shape, dt):
        h = self.stack.enter_context(self.nc.sbuf_tensor(name, list(shape), dt))
        return h

    def psum(self, name, shape=(128, 512), dt=F32):
        return self.stack.enter_context(
            self.nc.psum_tensor(name, list(shape), dt))

    def din(self, name, shape, dt=F32):
        return self.nc.dram_tensor(name, list(shape), dt, kind="ExternalInput")

    def dout(self, name, shape, dt=F32):
        return self.nc.dram_tensor(name, list(shape), dt, kind="ExternalOutput")

    def dint(self, name, shape, dt=F32):
        return self.nc.dram_tensor(name, list(shape), dt, kind="Internal")



def emit_ingest_transpose(ctx, block, *, x_dram, ident, xT_dram, xb_sb, psum, T=TPC):
    ld = ctx.sem("ing_ld")
    ldp = [ctx.sem("ing_ld0"), ctx.sem("ing_ld1")]
    tr = ctx.sem("ing_tr")
    ev = ctx.sem("ing_ev")
    xo = ctx.sem("ing_xo")
    xop = [ctx.sem("ing_xo0"), ctx.sem("ing_xo1")]
    cp = ctx.sem("ing_cp")
    xtok = ctx.sbuf("ing_xtok", [128, 2 * 1024], F32)
    idsb = ctx.sbuf("ing_id", [128, 128], F32)
    stg = ctx.sbuf("ing_stg", [128, 2 * 512], F32)
    ps = psum
    ntt = T // 128
    pitch_xb = xb_sb[:].ap[0][0]

    @block.sync
    def _(s_):
        ld.inc(s_.dma_start(out=idsb[:], in_=ident.ap()), 16)
        for tt in range(ntt):
            if tt >= 2:
                s_.wait_ge(tr.h, 2 * tt - 2)
            ldp[tt % 2].inc(s_.dma_start(
                out=xtok[:, (tt % 2) * 1024:(tt % 2) * 1024 + 1024],
                in_=x_dram.ap()[tt * 128:(tt + 1) * 128, :]), 16)

    @block.tensor
    def _(t):
        t.wait_ge(ld.h, 16)
        for tt in range(ntt):
            t.wait_ge(ldp[tt % 2].h, 16 * (tt // 2 + 1))
            for g in range(2):
                j = 2 * tt + g
                if j >= 2:
                    t.wait_ge(ev.h, j - 1)
                for c4 in range(4):
                    c = g * 4 + c4
                    ins = t.transpose(
                        ps[j % 2][:, c4 * 128:c4 * 128 + 128],
                        xtok[:, (tt % 2) * 1024 + c * 128:(tt % 2) * 1024 + (c + 1) * 128],
                        idsb[:])
                tr.inc(ins, 1)

    @block.vector
    def _(v):
        for tt in range(ntt):
            for g in range(2):
                j = 2 * tt + g
                v.wait_ge(tr.h, j + 1)
                if j >= 2:
                    v.wait_ge(xop[j % 2].h, 16 * (j // 2))
                ins = v.tensor_copy(stg[:, (j % 2) * 512:(j % 2) * 512 + 512],
                                    ps[j % 2][:, 0:512])
                ev.inc(ins, 1)

    @block.scalar
    def _(sc):
        for tt in range(ntt):
            for g in range(2):
                j = 2 * tt + g
                sc.wait_ge(ev.h, j + 1)
                dst = dap(xb_sb, (g * 4) * T + tt * 128,
                          [[pitch_xb, 128], [T, 4], [1, 128]])
                ins = sc.activation(dst, stg[:, (j % 2) * 512:(j % 2) * 512 + 512],
                                    AF.Copy)
                cp.inc(ins, 1)
                dst2 = dap(xT_dram, (g * 4) * 128 * T + tt * 128,
                           [[T, 128], [128 * T, 4], [1, 128]])
                xop[j % 2].inc(
                    sc.dma_start(out=dst2,
                                 in_=stg[:, (j % 2) * 512:(j % 2) * 512 + 512]),
                    16)


def emit_gemm(ctx, block, *, tag, xb_sb, xb_T, w_dram, w_cols, n_in_ch,
              n_out_ch, psum, evac, t_tile=512, tensor_guards=(),
              oc_stride=1):
    ld0 = ctx.sem(tag + "_wld0")
    ld1 = ctx.sem(tag + "_wld1")
    lds = [ld0, ld1]
    mm = ctx.sem(tag + "_mm")
    ev = ctx.sem(tag + "_ev")
    wbuf = ctx.sbuf(tag + "_wbuf", [128, 2 * n_in_ch * 128], BF16)
    pitch_w = wbuf[:].ap[0][0]
    n_tt = xb_T // t_tile
    nb = len(psum)

    @block.gpsimd
    def _(g):
        for i in range(n_out_ch):
            if i >= 2:
                g.wait_ge(mm.h, (i - 1) * n_tt)
            src = dap(w_dram, i * 128 * oc_stride,
                      [[w_cols, 128], [128 * w_cols, n_in_ch], [1, 128]])
            dst = dap(wbuf, (i % 2) * n_in_ch * 128,
                      [[pitch_w, 128], [128, n_in_ch], [1, 128]])
            lds[i % 2].inc(g.dma_start(out=dst, in_=src), 16)

    @block.tensor
    def _(t):
        for s in tensor_guards:
            s.wait(t)
        for i in range(n_out_ch):
            t.wait_ge(lds[i % 2].h, 16 * (i // 2 + 1))
            base = (i % 2) * n_in_ch * 128
            for tt in range(n_tt):
                j = i * n_tt + tt
                if j >= nb:
                    t.wait_ge(ev.h, j - nb + 1)
                for ic in range(n_in_ch):
                    ins = t.matmul(
                        psum[j % nb][:, 0:t_tile],
                        wbuf[:, base + ic * 128: base + ic * 128 + 128],
                        xb_sb[:, ic * xb_T + tt * t_tile: ic * xb_T + (tt + 1) * t_tile],
                        start=(ic == 0), stop=(ic == n_in_ch - 1))
                mm.inc(ins, 1)

    for i in range(n_out_ch):
        for tt in range(n_tt):
            j = i * n_tt + tt
            evac(i, tt, psum[j % nb][:, 0:t_tile], j)
    return mm, ev


def emit_norm2(ctx, block, *, tag, src_bf, T, n_ch, sq_bf, onescol_bf, psum,
               out_row_sb, out_off, pre_guards=(), bank_guards=()):
    sq = ctx.sem(tag + "_sq")
    mm = ctx.sem(tag + "_mm")
    ev = ctx.sem(tag + "_ev")
    n_tt = T // 512
    nb = len(psum)

    @block.scalar
    def _(sc):
        for s in pre_guards:
            s.wait(sc)
        for c in range(n_ch):
            ins = sc.activation(sq_bf[:, c * T:(c + 1) * T],
                                src_bf[:, c * T:(c + 1) * T], AF.Square)
            sq.inc(ins, 1)

    @block.tensor
    def _(t):
        sq.wait(t)
        for s in bank_guards:
            s.wait(t)
        for tt in range(n_tt):
            if tt >= nb:
                t.wait_ge(ev.h, tt - nb + 1)
            for c in range(n_ch):
                ins = t.matmul(psum[tt % nb][0:1, 0:512],
                               onescol_bf[:, 0:1],
                               sq_bf[:, c * T + tt * 512: c * T + tt * 512 + 512],
                               start=(c == 0), stop=(c == n_ch - 1))
            mm.inc(ins, 1)

    @block.vector
    def _(v):
        for tt in range(n_tt):
            v.wait_ge(mm.h, mm.n - (n_tt - 1 - tt))
            ins = v.tensor_copy(
                out_row_sb[0:1, out_off + tt * 512:out_off + tt * 512 + 512],
                psum[tt % nb][0:1, 0:512])
            ev.inc(ins, 1)
    return ev



def build_L0a():
    ctx = LaunchCtx("L0a")
    nc = ctx.nc
    x_in = ctx.din("x_in", [TPC, D])
    ident = ctx.din("ident", [128, 128])
    wq = ctx.din("wq", [D, D])
    wk = ctx.din("wk", [D, D])
    wv = ctx.din("wv", [D, D])
    bqkv = ctx.din("bqkv", [3, D])
    xT_o = ctx.dout("xT", [D, TPC])
    qT_o = ctx.dout("qT", [D, TPC])
    kT_o = ctx.dout("kT", [D, TPC])
    vT_o = ctx.dout("vT", [D, TPC])
    nrm_o = ctx.dout("nrm", [2 * TPC])

    with ctx.stack:
        xb = ctx.sbuf("xb", [128, NCH * TPC], BF16)
        qT = ctx.sbuf("qT_sb", [128, NCH * TPC], BF16)
        kT = ctx.sbuf("kT_sb", [128, NCH * TPC], BF16)
        vT = ctx.sbuf("vT_sb", [128, NCH * TPC], BF16)
        sq = ctx.sbuf("sq", [128, NCH * TPC], BF16)
        bias = ctx.sbuf("bias", [128, 3 * NCH], F32)
        onescol = ctx.sbuf("onescol", [128, 1], BF16)
        nrow = ctx.sbuf("nrow", [1, 2 * TPC], F32)
        ps = [ctx.psum("ps%d" % i) for i in range(8)]

        with nc.Block() as block:
            bld = ctx.sem("bld")
            ini = ctx.sem("ini")

            @block.gpsimd
            def _(g):
                ini.inc(g.memset(onescol[:], 1.0), 1)
                for j in range(3 * NCH):
                    src = dap(bqkv, j * 128, [[1, 128], [1, 1]])
                    bld.inc(g.dma_start(out=bias[:, j:j + 1], in_=src), 16)

            bsc = ctx.sem("bsc")

            @block.scalar
            def _(sc):
                bld.wait(sc)
                ins = sc.activation(bias[:, 0:NCH], bias[:, 0:NCH], AF.Copy,
                                    scale=1.0 / 32.0)
                bsc.inc(ins, 1)

            emit_ingest_transpose(ctx, block, x_dram=x_in, ident=ident,
                                  xT_dram=xT_o, xb_sb=xb, psum=ps[0:2])
            ing_cp = ctx.sems["ing_cp"]

            def mk_evac(tag, dst_sb, bias_col0, scale):
                mmS = ctx.sem(tag + "_mm")
                evS = ctx.sem(tag + "_ev")

                def evac(oc, tt, ps_ap, j):
                    @block.scalar
                    def _(sc):
                        if j == 0:
                            bld.wait(sc)
                            bsc.wait(sc)
                        sc.wait_ge(mmS.h, j + 1)
                        ins = sc.activation(
                            dst_sb[:, oc * TPC + tt * 512: oc * TPC + tt * 512 + 512],
                            ps_ap, AF.Identity,
                            bias=bias[:, bias_col0 + oc:bias_col0 + oc + 1],
                            scale=scale)
                        evS.inc(ins, 1)
                return evac

            emit_gemm(ctx, block, tag="gq", xb_sb=xb, xb_T=TPC, w_dram=wq,
                      w_cols=D, n_in_ch=NCH, n_out_ch=NCH, psum=ps[0:4],
                      evac=mk_evac("gq", qT, 0, 1.0 / 32.0),
                      tensor_guards=(ing_cp,))
            emit_gemm(ctx, block, tag="gk", xb_sb=xb, xb_T=TPC, w_dram=wk,
                      w_cols=D, n_in_ch=NCH, n_out_ch=NCH, psum=ps[4:8],
                      evac=mk_evac("gk", kT, NCH, 1.0))
            emit_gemm(ctx, block, tag="gv", xb_sb=xb, xb_T=TPC, w_dram=wv,
                      w_cols=D, n_in_ch=NCH, n_out_ch=NCH, psum=ps[0:4],
                      evac=mk_evac("gv", vT, 2 * NCH, 1.0),
                      tensor_guards=(ctx.sems["gq_ev"],))

            nq_ev = emit_norm2(ctx, block, tag="nq", src_bf=qT, T=TPC,
                               n_ch=NCH, sq_bf=sq, onescol_bf=onescol,
                               psum=ps[4:6], out_row_sb=nrow, out_off=0,
                               pre_guards=(ctx.sems["gq_ev"],),
                               bank_guards=(ctx.sems["gk_ev"],))
            nk_ev = emit_norm2(ctx, block, tag="nk", src_bf=kT, T=TPC,
                               n_ch=NCH, sq_bf=sq, onescol_bf=onescol,
                               psum=ps[6:8], out_row_sb=nrow, out_off=TPC,
                               pre_guards=(ctx.sems["gk_ev"],
                                           ctx.sems["nq_mm"]),
                               bank_guards=())

            st = ctx.sem("st")

            @block.gpsimd
            def _(g):
                for tag2, src_sb, dst in (("gq", qT, qT_o), ("gk", kT, kT_o),
                                          ("gv", vT, vT_o)):
                    ctx.sems[tag2 + "_ev"].wait(g)
                    dstap = dap(dst, 0, [[TPC, 128], [128 * TPC, NCH], [1, TPC]])
                    srcap = src_sb[:].rearrange("p (c t) -> p c t", c=NCH)
                    st.inc(g.dma_start(out=dstap, in_=srcap), 16)
                nq_ev.wait(g)
                nk_ev.wait(g)
                st.inc(g.dma_start(
                    out=dap(nrm_o, 0, [[2 * TPC, 1], [1, 2 * TPC]]),
                    in_=nrow[0:1, :]), 16)
                ctx.sems["ing_xo0"].wait(g)
                ctx.sems["ing_xo1"].wait(g)
                st.wait(g)
    return nc



_I128 = np.eye(128, dtype=np.float32)


def _core_slices(full):
    out = []
    for c in range(NCORE):
        b, r = c // 4, c % 4
        out.append(np.ascontiguousarray(full[b, r * TPC:(r + 1) * TPC]))
    return out


def run_L0a(inputs):
    nc = build_L0a()
    xs = _core_slices(inputs["L0_attn_x"])
    bqkv = np.stack([inputs["L0_attn_bq"], inputs["L0_attn_bk"],
                     inputs["L0_attn_bv"]]).astype(np.float32)
    in_maps = []
    for c in range(NCORE):
        in_maps.append({
            "x_in": xs[c], "ident": _I128,
            "wq": np.asarray(inputs["L0_attn_Wq"], np.float32),
            "wk": np.asarray(inputs["L0_attn_Wk"], np.float32),
            "wv": np.asarray(inputs["L0_attn_Wv"], np.float32),
            "bqkv": bqkv,
        })
    res = run_bass_kernel_spmd(nc, in_maps, core_ids=list(range(NCORE)))
    return res.results





def build_L0b():
    ctx = LaunchCtx("L0b")
    nc = ctx.nc
    qT_i = ctx.din("qT_i", [D, TPC])
    kT_i = ctx.din("kT_i", [D, S])
    v_i = ctx.din("v_i", [S, D])
    nq_i = ctx.din("nq_i", [TPC])
    nk_i = ctx.din("nk_i", [S])
    onesr = ctx.din("onesr", [1, 128])
    aT_o = ctx.dout("aT", [D, TPC])

    NKC = S // 128
    NQT = TPC // 512

    with ctx.stack:
        qb = ctx.sbuf("qb", [128, NCH * TPC], BF16)
        Eb = ctx.sbuf("Eb", [128, NKC * 512], BF16)
        kbuf = ctx.sbuf("kbuf", [128, 2 * NCH * 128], BF16)
        vbuf = ctx.sbuf("vbuf", [128, 2 * NKC * 128], BF16)
        onescol = ctx.sbuf("onescol", [128, 1], BF16)
        onesr_sb = ctx.sbuf("onesr_sb", [1, 128], F32)
        nrows = ctx.sbuf("nrows", [1, TPC + S], F32)
        mrow = ctx.sbuf("mrow", [1, 8], F32)
        mcol = ctx.sbuf("mcol", [128, 1], F32)
        rrow = ctx.sbuf("rrow", [1, 512], F32)
        rb = ctx.sbuf("rb", [128, 512], F32)
        aT_stg = ctx.sbuf("aT_stg", [128, NCH * 512], F32)
        ps = [ctx.psum("ps%d" % i) for i in range(8)]

        with nc.Block() as block:
            cst = ctx.sem("cst")
            qld = ctx.sem("qld")
            kldp = [ctx.sem("kld0"), ctx.sem("kld1")]
            vldp = [ctx.sem("vld0"), ctx.sem("vld1")]
            smm = ctx.sem("smm")
            sex = ctx.sem("sex")
            summ = ctx.sem("summ")
            rcp = ctx.sem("rcp")
            pvm = ctx.sem("pvm")
            nev = ctx.sem("nev")
            sto = ctx.sem("sto")

            @block.sync
            def _(s_):
                cst.inc(s_.dma_start(out=onesr_sb[:], in_=onesr.ap()), 16)
                cst.inc(s_.dma_start(out=nrows[0:1, 0:TPC],
                                     in_=dap(nq_i, 0, [[TPC, 1], [1, TPC]])), 16)
                cst.inc(s_.dma_start(out=nrows[0:1, TPC:TPC + S],
                                     in_=dap(nk_i, 0, [[S, 1], [1, S]])), 16)

            @block.gpsimd
            def _(g):
                cst.inc(g.memset(onescol[:], 1.0), 1)
                src = dap(qT_i, 0, [[TPC, 128], [128 * TPC, NCH], [1, TPC]])
                dst = qb[:].rearrange("p (c t) -> p c t", c=NCH)
                qld.inc(g.dma_start(out=dst, in_=src), 16)

            @block.vector
            def _(v):
                cst.wait(v)
                v.reduce_max(mrow[0:1, 0:1], nrows[0:1, 0:TPC], axis=AX.X)
                ins = v.reduce_max(mrow[0:1, 1:2], nrows[0:1, TPC:TPC + S],
                                   axis=AX.X)
                cst.inc(ins, 1)

            @block.scalar
            def _(sc):
                cst.wait(sc)
                ins = sc.activation(mrow[0:1, 2:4], mrow[0:1, 0:2], AF.Sqrt)
                cst.inc(ins, 1)

            @block.vector
            def _(v):
                cst.wait(v)
                ins = v.tensor_mul(mrow[0:1, 4:5], mrow[0:1, 2:3], mrow[0:1, 3:4])
                cst.inc(ins, 1)

            @block.scalar
            def _(sc):
                cst.wait(sc)
                ins = sc.activation(mrow[0:1, 5:6], mrow[0:1, 4:5], AF.Copy,
                                    scale=-1.0)
                cst.inc(ins, 1)

            @block.tensor
            def _(t):
                cst.wait(t)
                ins = t.matmul(ps[7][:, 0:1], onesr_sb[0:1, 0:128],
                               mrow[0:1, 5:6], start=True, stop=True)
                cst.inc(ins, 1)

            @block.vector
            def _(v):
                cst.wait(v)
                ins = v.tensor_copy(mcol[:, 0:1], ps[7][:, 0:1])
                cst.inc(ins, 1)

            for qt in range(NQT):
                @block.gpsimd
                def _(g, qt=qt):
                    for kc in range(NKC):
                        j = qt * NKC + kc
                        if j >= 2:
                            g.wait_ge(smm.h, j - 1)
                        src = dap(kT_i, kc * 128,
                                  [[S, 128], [128 * S, NCH], [1, 128]])
                        dst = dap(kbuf, (j % 2) * NCH * 128,
                                  [[2 * NCH * 128, 128], [128, NCH], [1, 128]])
                        kldp[j % 2].inc(g.dma_start(out=dst, in_=src), 16)
                    for oc in range(NCH):
                        need = qt * NCH + oc - 1
                        if need >= 0:
                            g.wait_ge(pvm.h, need)
                        src = dap(v_i, oc * 128,
                                  [[D, 128], [128 * D, NKC], [1, 128]])
                        dst = dap(vbuf, (oc % 2) * NKC * 128,
                                  [[2 * NKC * 128, 128], [128, NKC], [1, 128]])
                        vldp[oc % 2].inc(g.dma_start(out=dst, in_=src), 16)

                @block.tensor
                def _(t, qt=qt):
                    if qt == 0:
                        cst.wait(t)
                        t.wait_ge(qld.h, 16)
                    for kc in range(NKC):
                        j = qt * NKC + kc
                        t.wait_ge(kldp[j % 2].h, 16 * (j // 2 + 1))
                        if j >= 2:
                            t.wait_ge(sex.h, j - 1)
                        base = (j % 2) * NCH * 128
                        for ic in range(NCH):
                            ins = t.matmul(
                                ps[6 + (j % 2)][:, 0:512],
                                kbuf[:, base + ic * 128: base + ic * 128 + 128],
                                qb[:, ic * TPC + qt * 512: ic * TPC + qt * 512 + 512],
                                start=(ic == 0), stop=(ic == NCH - 1))
                        smm.inc(ins, 1)

                @block.scalar
                def _(sc, qt=qt):
                    if qt > 0:
                        sc.wait_ge(pvm.h, 8 * qt)
                    for kc in range(NKC):
                        j = qt * NKC + kc
                        sc.wait_ge(smm.h, j + 1)
                        ins = sc.activation(
                            Eb[:, kc * 512:(kc + 1) * 512],
                            ps[6 + (j % 2)][:, 0:512], AF.Exp,
                            bias=mcol[:, 0:1])
                        sex.inc(ins, 1)

                @block.tensor
                def _(t, qt=qt):
                    t.wait_ge(sex.h, (qt + 1) * NKC)
                    if qt > 0:
                        t.wait_ge(rcp.h, 2 * qt - 1)
                    for kc in range(NKC):
                        ins = t.matmul(ps[4][0:1, 0:512], onescol[:, 0:1],
                                       Eb[:, kc * 512:(kc + 1) * 512],
                                       start=(kc == 0), stop=(kc == NKC - 1))
                    summ.inc(ins, 1)

                @block.vector
                def _(v, qt=qt):
                    v.wait_ge(summ.h, 2 * qt + 1)
                    ins = v.reciprocal(rrow[0:1, 0:512], ps[4][0:1, 0:512])
                    rcp.inc(ins, 1)

                @block.tensor
                def _(t, qt=qt):
                    t.wait_ge(rcp.h, 2 * qt + 1)
                    ins = t.matmul(ps[5][:, 0:512], onesr_sb[0:1, 0:128],
                                   rrow[0:1, 0:512], start=True, stop=True)
                    summ.inc(ins, 1)

                @block.vector
                def _(v, qt=qt):
                    v.wait_ge(summ.h, 2 * qt + 2)
                    ins = v.tensor_copy(rb[:, 0:512], ps[5][:, 0:512])
                    rcp.inc(ins, 1)

                @block.tensor
                def _(t, qt=qt):
                    for oc in range(NCH):
                        jv = qt * NCH + oc
                        t.wait_ge(vldp[jv % 2].h, 16 * (jv // 2 + 1))
                        if oc >= 4:
                            t.wait_ge(nev.h, qt * NCH + oc - 3)
                        elif qt > 0:
                            t.wait_ge(nev.h, (qt - 1) * NCH + oc + 5)
                        base = (oc % 2) * NKC * 128
                        bank = ps[oc % 4]
                        for kc in range(NKC):
                            ins = t.matmul(
                                bank[:, 0:512],
                                vbuf[:, base + kc * 128: base + kc * 128 + 128],
                                Eb[:, kc * 512:(kc + 1) * 512],
                                start=(kc == 0), stop=(kc == NKC - 1))
                        pvm.inc(ins, 1)

                @block.vector
                def _(v, qt=qt):
                    for oc in range(NCH):
                        v.wait_ge(pvm.h, qt * NCH + oc + 1)
                        v.wait_ge(rcp.h, 2 * qt + 2)
                        if qt > 0 and oc == 0:
                            v.wait_ge(sto.h, 16 * qt)
                        ins = v.tensor_mul(
                            aT_stg[:, oc * 512:(oc + 1) * 512],
                            ps[oc % 4][:, 0:512], rb[:, 0:512])
                        nev.inc(ins, 1)

                @block.sync
                def _(s_, qt=qt):
                    s_.wait_ge(nev.h, (qt + 1) * NCH)
                    dst = dap(aT_o, qt * 512,
                              [[TPC, 128], [128 * TPC, NCH], [1, 512]])
                    src = aT_stg[:].rearrange("p (c t) -> p c t", c=NCH)
                    sto.inc(s_.dma_start(out=dst, in_=src), 16)

            @block.sync
            def _(s_):
                sto.wait(s_)
    return nc


def run_L0b(qTs, kT_fulls, v_fulls, nqs, nk_fulls):
    nc = build_L0b()
    onesr = np.ones((1, 128), np.float32)
    in_maps = []
    for c in range(NCORE):
        b = c // 4
        in_maps.append({
            "qT_i": qTs[c], "kT_i": kT_fulls[b], "v_i": v_fulls[b],
            "nq_i": nqs[c], "nk_i": nk_fulls[b], "onesr": onesr,
        })
    res = run_bass_kernel_spmd(nc, in_maps, core_ids=list(range(NCORE)))
    return res.results


def emit_ln(ctx, block, *, tag, x_sb, T, n_ch, psum4, onescol32, onescol_bf,
            onesr32, sq_bf, rows, bcast, eps_ap, out_sb=None,
            gamma_beta=None, pre_guards=(), bank_guards=(), t_tile=512,
            x_off=0, out_off=0):
    out_sb = x_sb if out_sb is None else out_sb
    sq = ctx.sem(tag + "_sq")
    mm = ctx.sem(tag + "_mm")
    row = ctx.sem(tag + "_row")
    st_ = ctx.sem(tag + "_st")
    bmm = ctx.sem(tag + "_bmm")
    app = ctx.sem(tag + "_app")
    row_b, bmm_b, app_b = row.n, bmm.n, app.n
    n_tt = T // t_tile
    per_tt = n_ch * (3 if gamma_beta is not None else 2)
    D_loc = n_ch * 128

    def xs(c, tt):
        return x_sb[:, x_off + c * T + tt * t_tile: x_off + c * T + (tt + 1) * t_tile]

    @block.scalar
    def _(sc):
        for s in pre_guards:
            s.wait(sc)
        for c in range(n_ch):
            ins = sc.activation(sq_bf[:, c * T:(c + 1) * T],
                                x_sb[:, x_off + c * T: x_off + (c + 1) * T],
                                AF.Square)
        sq.inc(ins, 1)

    @block.tensor
    def _(t):
        sq.wait(t)
        for s in bank_guards:
            s.wait(t)
        for tt in range(n_tt):
            if tt >= 1:
                t.wait_ge(row.h, row_b + 2 * tt)
            for c in range(n_ch):
                i1 = t.matmul(psum4[0][0:1, 0:t_tile], onescol32[:, 0:1],
                              xs(c, tt), start=(c == 0), stop=(c == n_ch - 1))
            mm.inc(i1, 1)
            for c in range(n_ch):
                i2 = t.matmul(psum4[1][0:1, 0:t_tile], onescol_bf[:, 0:1],
                              sq_bf[:, c * T + tt * t_tile: c * T + (tt + 1) * t_tile],
                              start=(c == 0), stop=(c == n_ch - 1))
            mm.inc(i2, 1)

    @block.vector
    def _(v):
        for tt in range(n_tt):
            v.wait_ge(mm.h, mm.n - 2 * (n_tt - 1 - tt))
            v.tensor_copy(rows[0:1, tt * t_tile:(tt + 1) * t_tile],
                          psum4[0][0:1, 0:t_tile])
            ins = v.tensor_copy(rows[0:1, T + tt * t_tile: T + (tt + 1) * t_tile],
                                psum4[1][0:1, 0:t_tile])
            row.inc(ins, 2)

    @block.scalar
    def _(sc):
        row.wait(sc)
        sc.activation(rows[0:1, 2 * T:3 * T], rows[0:1, 0:T], AF.Identity,
                      scale=1.0 / D_loc)
        ins = sc.activation(rows[0:1, 3 * T:4 * T], rows[0:1, T:2 * T],
                            AF.Identity, bias=eps_ap, scale=1.0 / D_loc)
        st_.inc(ins, 1)

    @block.vector
    def _(v):
        st_.wait(v)
        st_.inc(v.tensor_mul(rows[0:1, T:2 * T], rows[0:1, 2 * T:3 * T],
                             rows[0:1, 2 * T:3 * T]), 1)
        v.wait_ge(st_.h, st_.n)
        ins = v.tensor_sub(rows[0:1, 0:T], rows[0:1, 3 * T:4 * T],
                           rows[0:1, T:2 * T])
        st_.inc(ins, 1)

    @block.scalar
    def _(sc):
        st_.wait(sc)
        ins = sc.activation(rows[0:1, 3 * T:4 * T], rows[0:1, 0:T], AF.Sqrt)
        st_.inc(ins, 1)

    @block.vector
    def _(v):
        st_.wait(v)
        st_.inc(v.reciprocal(rows[0:1, T:2 * T], rows[0:1, 3 * T:4 * T]), 1)
        v.wait_ge(st_.h, st_.n)
        ins = v.tensor_mul(rows[0:1, 0:T], rows[0:1, 2 * T:3 * T],
                           rows[0:1, T:2 * T])
        st_.inc(ins, 1)

    for tt in range(n_tt):
        @block.tensor
        def _(t, tt=tt):
            st_.wait(t)
            if tt >= 1:
                t.wait_ge(bmm.h, bmm_b + 3 * tt)
            t.matmul(psum4[2][:, 0:t_tile], onesr32[0:1, 0:128],
                     rows[0:1, T + tt * t_tile: T + (tt + 1) * t_tile],
                     start=True, stop=True)
            ins = t.matmul(psum4[3][:, 0:t_tile], onesr32[0:1, 0:128],
                           rows[0:1, tt * t_tile:(tt + 1) * t_tile],
                           start=True, stop=True)
            bmm.inc(ins, 2)

        @block.vector
        def _(v, tt=tt):
            v.wait_ge(bmm.h, bmm_b + 2 + 3 * tt)
            if tt > 0:
                v.wait_ge(app.h, app_b + per_tt * tt)
            v.tensor_copy(bcast[:, 0:t_tile], psum4[2][:, 0:t_tile])
            ins = v.tensor_copy(bcast[:, t_tile:2 * t_tile],
                                psum4[3][:, 0:t_tile])
            bmm.inc(ins, 1)

            v.wait_ge(bmm.h, bmm_b + 3 * (tt + 1))

            def dsl(c):
                return out_sb[:, out_off + c * T + tt * t_tile:
                              out_off + c * T + (tt + 1) * t_tile]
            for c0 in range(0, n_ch, 2):
                cs = [c0] if c0 + 1 >= n_ch else [c0, c0 + 1]
                for c in cs:
                    app.inc(v.tensor_mul(dsl(c), xs(c, tt), bcast[:, 0:t_tile]), 1)
                v.wait_ge(app.h, app.n)
                for c in cs:
                    app.inc(v.tensor_sub(dsl(c), dsl(c),
                                         bcast[:, t_tile:2 * t_tile]), 1)
                if gamma_beta is not None:
                    v.wait_ge(app.h, app.n)
                    gsb, gcol0, bcol0 = gamma_beta
                    for c in cs:
                        ins = v.tensor_scalar(dsl(c), dsl(c),
                                              gsb[:, gcol0 + c:gcol0 + c + 1],
                                              gsb[:, bcol0 + c:bcol0 + c + 1],
                                              op0=ALU.mult, op1=ALU.add)
                        app.inc(ins, 1)
    return app



def build_L0c():
    ctx = LaunchCtx("L0c")
    nc = ctx.nc
    aT_i = ctx.din("aT_i", [D, TPC])
    xT_i = ctx.din("xT_i", [D, TPC])
    wo = ctx.din("wo", [D, D])
    bo = ctx.din("bo", [D])
    onesr = ctx.din("onesr", [1, 128])
    x1_o = ctx.dout("x1T", [D, TPC])

    with ctx.stack:
        ab = ctx.sbuf("ab", [128, NCH * TPC], BF16)
        z = ctx.sbuf("z", [128, NCH * TPC], F32)
        sqs = ctx.sbuf("sqs", [128, NCH * TPC], BF16)
        bias = ctx.sbuf("bias", [128, NCH], F32)
        onescol32 = ctx.sbuf("onescol32", [128, 1], F32)
        onescol_bf = ctx.sbuf("onescol_bf", [128, 1], BF16)
        onesr_sb = ctx.sbuf("onesr_sb", [1, 128], F32)
        rows = ctx.sbuf("rows", [1, 4 * TPC], F32)
        bcast = ctx.sbuf("bcast", [128, 2 * 512], F32)
        epsr = ctx.sbuf("epsr", [1, 1], F32)
        ps = [ctx.psum("ps%d" % i) for i in range(8)]

        with nc.Block() as block:
            bld = ctx.sem("bld")
            ald = ctx.sem("ald")
            xld = ctx.sem("xld")
            ini = ctx.sem("ini")

            @block.gpsimd
            def _(g):
                ini.inc(g.memset(onescol32[:], 1.0), 1)
                ini.inc(g.memset(onescol_bf[:], 1.0), 1)
                ini.inc(g.memset(epsr[:], 1e-5), 1)
                for j in range(NCH):
                    bld.inc(g.dma_start(out=bias[:, j:j + 1],
                                        in_=dap(bo, j * 128, [[1, 128], [1, 1]])), 16)
                src = dap(aT_i, 0, [[TPC, 128], [128 * TPC, NCH], [1, TPC]])
                ald.inc(g.dma_start(out=ab[:].rearrange("p (c t) -> p c t", c=NCH),
                                    in_=src), 16)

            orl0 = ctx.sem("orl0")

            @block.sync
            def _(s_):
                orl0.inc(s_.dma_start(out=onesr_sb[:], in_=onesr.ap()), 16)
                src = dap(xT_i, 0, [[TPC, 128], [128 * TPC, NCH], [1, TPC]])
                xld.inc(s_.dma_start(out=z[:].rearrange("p (c t) -> p c t", c=NCH),
                                     in_=src), 16)

            gmm = ctx.sem("go_mm")
            gev = ctx.sem("go_ev")

            def evac(oc, tt, ps_ap, j):
                @block.vector
                def _(v):
                    v.wait_ge(gmm.h, j + 1)
                    if j == 0:
                        bld.wait(v)
                        xld.wait(v)
                        ini.wait(v)
                    dst = z[:, oc * TPC + tt * 512: oc * TPC + tt * 512 + 512]
                    ins = v.scalar_tensor_tensor(dst, ps_ap, bias[:, oc:oc + 1],
                                                 dst, op0=ALU.add, op1=ALU.add)
                    gev.inc(ins, 1)

            def wait_a(t):
                ald.wait(t)

            emit_gemm(ctx, block, tag="go", xb_sb=ab, xb_T=TPC, w_dram=wo,
                      w_cols=D, n_in_ch=NCH, n_out_ch=NCH, psum=ps[0:4],
                      evac=evac, tensor_guards=(ald,))

            app = emit_ln(ctx, block, tag="ln", x_sb=z, T=TPC, n_ch=NCH,
                          psum4=ps[4:8], onescol32=onescol32,
                          onescol_bf=onescol_bf, onesr32=onesr_sb,
                          sq_bf=sqs, rows=rows, bcast=bcast,
                          eps_ap=epsr[0:1, 0:1],
                          pre_guards=(ctx.sems["go_ev"],),
                          bank_guards=(bld, ini, orl0))

            sto = ctx.sem("sto")

            @block.sync
            def _(s_):
                app.wait(s_)
                dst = dap(x1_o, 0, [[TPC, 128], [128 * TPC, NCH], [1, TPC]])
                sto.inc(s_.dma_start(out=dst,
                                     in_=z[:].rearrange("p (c t) -> p c t", c=NCH)),
                        16)
                sto.wait(s_)
    return nc


def run_L0c(aTs, xTs, inputs):
    nc = build_L0c()
    onesr = np.ones((1, 128), np.float32)
    in_maps = []
    for c in range(NCORE):
        in_maps.append({
            "aT_i": aTs[c], "xT_i": xTs[c],
            "wo": np.asarray(inputs["L0_attn_Wo"], np.float32),
            "bo": np.asarray(inputs["L0_attn_bo"], np.float32),
            "onesr": onesr,
        })
    res = run_bass_kernel_spmd(nc, in_maps, core_ids=list(range(NCORE)))
    return res.results


def build_L0de():
    ctx = LaunchCtx("L0de")
    nc = ctx.nc
    x1_i = ctx.din("x1_i", [D, TPC])
    maker = ctx.din("maker", [4, 64, 16, 16])
    upt = ctx.din("upt", [4, 64, 16, 64])
    cpT = ctx.din("cpT", [16384, D])
    onesr = ctx.din("onesr", [1, 128])
    x2_o = ctx.dout("x2T", [D, TPC])

    FH = 16384
    NZC = 32
    TQ = 256
    NTQ = TPC // TQ

    hT_int = ctx.dint("hT_int", [FH, TPC], BF16)
    cpb_int = ctx.dint("cpb_int", [8 * 128 * 128 * 128], BF16)

    with ctx.stack:
        arena = ctx.sbuf("arena", [128, 32768], BF16)
        wbuf = ctx.sbuf("wbuf", [128, 2 * 128 * 128], BF16)
        bd1 = ctx.sbuf("bd1", [128, 8 * 128], BF16)
        bd2 = ctx.sbuf("bd2", [128, 2 * 128], BF16)
        hstg = ctx.sbuf("hstg", [128, 2 * 2048], BF16)
        zq = ctx.sbuf("zq", [128, NCH * TQ], F32)
        x1q = ctx.sbuf("x1q", [128, NCH * TQ], F32)
        sqs = ctx.sbuf("sqs", [128, NCH * TQ], BF16)
        rows = ctx.sbuf("rows", [1, 4 * TQ], F32)
        bcast = ctx.sbuf("bcast", [128, 2 * TQ], F32)
        onescol32 = ctx.sbuf("onescol32", [128, 1], F32)
        onescol_bf = ctx.sbuf("onescol_bf", [128, 1], BF16)
        onesr_sb = ctx.sbuf("onesr_sb", [1, 128], F32)
        epsr = ctx.sbuf("epsr", [1, 1], F32)
        ps = [ctx.psum("ps%d" % i) for i in range(8)]

        with nc.Block() as block:
            cst = ctx.sem("cst")
            ini = ctx.sem("ini")
            cpc = ctx.sem("cpc")
            x1l = ctx.sem("x1l")
            bdl = ctx.sem("bdl")
            ymm = ctx.sem("ymm")
            yev = ctx.sem("yev")
            zmm = ctx.sem("zmm")
            zev = ctx.sem("zev")
            hsop = [ctx.sem("hso0"), ctx.sem("hso1")]

            @block.sync
            def _(s_):
                cst.inc(s_.dma_start(out=onesr_sb[:], in_=onesr.ap()), 16)

            @block.gpsimd
            def _(g):
                ini.inc(g.memset(onescol32[:], 1.0), 1)
                ini.inc(g.memset(onescol_bf[:], 1.0), 1)
                ini.inc(g.memset(epsr[:], 1e-5), 1)
                src = dap(x1_i, 0, [[TPC, 128], [128 * TPC, NCH], [1, TPC]])
                dst = dap(arena, 0, [[32768, 128], [TPC, NCH], [1, TPC]])
                x1l.inc(g.dma_start(out=dst, in_=src), 16)
                for oc in range(NCH):
                    for ih in range(2):
                        dst = dap(cpb_int, oc * 128 * 16384 + ih * 64 * 128,
                                  [[16384, 128], [128, 64], [1, 128]])
                        src = dap(cpT, ih * 64 * 128 * D + oc * 128,
                                  [[D, 128], [128 * D, 64], [1, 128]])
                        cpc.inc(g.dma_start(out=dst, in_=src), 16)

            for h in range(4):
                @block.gpsimd
                def _(g, h=h):
                    if h > 0:
                        g.wait_ge(ymm.h, 32 * h)
                    ini.inc(g.memset(bd1[:], 0.0), 1)
                    g.wait_ge(ini.h, ini.n)
                    for c in range(NCH):
                        for m in range(8):
                            r = 8 * c + m
                            src = dap(maker, (h * 64 + r) * 256,
                                      [[16, 16], [1, 16]])
                            dst = dap(bd1, 16 * m * 1024 + c * 128 + 16 * m,
                                      [[1024, 16], [1, 16]])
                            bdl.inc(g.dma_start(out=dst, in_=src), 16)

                @block.tensor
                def _(t, h=h):
                    if h == 0:
                        t.wait_ge(x1l.h, 16)
                    t.wait_ge(bdl.h, bdl.n)
                    for c in range(NCH):
                        for tt in range(4):
                            j = 32 * h + 4 * c + tt
                            if j >= 4:
                                t.wait_ge(yev.h, j - 3)
                            ins = t.matmul(
                                ps[j % 4][:, 0:512],
                                bd1[:, c * 128:(c + 1) * 128],
                                dap(arena, c * TPC + tt * 512,
                                    [[32768, 128], [1, 512]]),
                                start=True, stop=True)
                            ymm.inc(ins, 1)

                @block.vector
                def _(v, h=h):
                    if h > 0:
                        v.wait_ge(zmm.h, 128 * h)
                    for c in range(NCH):
                        for tt in range(4):
                            j = 32 * h + 4 * c + tt
                            v.wait_ge(ymm.h, j + 1)
                            ins = v.tensor_copy(
                                dap(arena, 16384 + c * TPC + tt * 512,
                                    [[32768, 128], [1, 512]]),
                                ps[j % 4][:, 0:512])
                            yev.inc(ins, 1)

                for zc in range(NZC):
                    @block.gpsimd
                    def _(g, h=h, zc=zc):
                        gi = 32 * h + zc
                        if gi >= 2:
                            g.wait_ge(zmm.h, (gi - 1) * 4)
                        base = (gi % 2) * 128
                        ini.inc(g.memset(bd2[:, base:base + 128], 0.0), 1)
                        g.wait_ge(ini.h, ini.n)
                        for s in range(2):
                            r = 2 * zc + s
                            m = r % 8
                            src = dap(upt, (h * 64 + r) * 1024,
                                      [[64, 16], [1, 64]])
                            dst = dap(bd2, 16 * m * 256 + base + 64 * s,
                                      [[256, 16], [1, 64]])
                            bdl.inc(g.dma_start(out=dst, in_=src), 16)

                    @block.tensor
                    def _(t, h=h, zc=zc):
                        gi = 32 * h + zc
                        t.wait_ge(bdl.h, bdl.n)
                        if zc == 0:
                            t.wait_ge(yev.h, 32 * (h + 1))
                        yc = zc // 4
                        base = (gi % 2) * 128
                        for tt in range(4):
                            j = 4 * gi + tt
                            if j >= 4:
                                t.wait_ge(zev.h, j - 3)
                            ins = t.matmul(
                                ps[4 + j % 4][:, 0:512],
                                bd2[:, base:base + 128],
                                dap(arena, 16384 + yc * TPC + tt * 512,
                                    [[32768, 128], [1, 512]]),
                                start=True, stop=True)
                            zmm.inc(ins, 1)

                    @block.scalar
                    def _(sc, h=h, zc=zc):
                        gi = 32 * h + zc
                        if gi >= 2:
                            sc.wait_ge(hsop[gi % 2].h, 16 * (gi // 2))
                        for tt in range(4):
                            j = 4 * gi + tt
                            sc.wait_ge(zmm.h, j + 1)
                            ins = sc.activation(
                                hstg[:, (gi % 2) * 2048 + tt * 512:
                                     (gi % 2) * 2048 + (tt + 1) * 512],
                                ps[4 + j % 4][:, 0:512], AF.Gelu)
                            zev.inc(ins, 1)

                    @block.sync
                    def _(s_, h=h, zc=zc):
                        gi = 32 * h + zc
                        s_.wait_ge(zev.h, 4 * (gi + 1))
                        F = h * 4096 + zc * 128
                        hsop[gi % 2].inc(s_.dma_start(
                            out=hT_int.ap()[F:F + 128, :],
                            in_=hstg[:, (gi % 2) * 2048:(gi % 2) * 2048 + 2048]),
                            16)

            hbl = ctx.sem("hbl")
            x1ql = ctx.sem("x1ql")
            wldp = [ctx.sem("wld0"), ctx.sem("wld1")]
            gmm = ctx.sem("gmm")
            gev = ctx.sem("gev")
            sto = ctx.sem("sto")
            prev_app = None
            for tq in range(NTQ):
                gmm_b, gev_b, sto_b = gmm.n, gev.n, sto.n
                hbl_b, x1ql_b = hbl.n, x1ql.n
                wld_b2 = [wldp[0].n, wldp[1].n]

                @block.sync
                def _(s_, tq=tq, gmm_b=gmm_b, gev_b=gev_b):
                    if tq == 0:
                        s_.wait_ge(hsop[0].h, 16 * 64)
                        s_.wait_ge(hsop[1].h, 16 * 64)
                    else:
                        s_.wait_ge(gmm.h, gmm_b)
                        s_.wait_ge(gev.h, gev_b)
                    src = dap(hT_int, tq * TQ,
                              [[TPC, 128], [128 * TPC, 128], [1, TQ]])
                    dst = dap(arena, 0, [[32768, 128], [TQ, 128], [1, TQ]])
                    hbl.inc(s_.dma_start(out=dst, in_=src), 16)
                    src2 = dap(x1_i, tq * TQ,
                               [[TPC, 128], [128 * TPC, NCH], [1, TQ]])
                    x1ql.inc(s_.dma_start(
                        out=x1q[:].rearrange("p (c t) -> p c t", c=NCH),
                        in_=src2), 16)

                @block.gpsimd
                def _(g, tq=tq, gmm_b=gmm_b):
                    for oc in range(NCH):
                        w = gmm_b + 128 * (oc - 1)
                        if w > 0:
                            g.wait_ge(gmm.h, w)
                        src = dap(cpb_int, oc * 128 * 16384,
                                  [[16384, 128], [128, 128], [1, 128]])
                        dst = dap(wbuf, (oc % 2) * 128 * 128,
                                  [[2 * 128 * 128, 128], [128, 128], [1, 128]])
                        wldp[oc % 2].inc(g.dma_start(out=dst, in_=src), 16)

                @block.tensor
                def _(t, tq=tq, gmm_b=gmm_b, gev_b=gev_b, hbl_b=hbl_b,
                      wld_b2=wld_b2):
                    t.wait_ge(hbl.h, hbl_b + 16)
                    if tq == 0:
                        t.wait_ge(zev.h, zev.n)
                        t.wait_ge(yev.h, yev.n)
                        cst.wait(t)
                        t.wait_ge(cpc.h, cpc.n)
                    for oc in range(NCH):
                        t.wait_ge(wldp[oc % 2].h, wld_b2[oc % 2] + 16 * (oc // 2 + 1))
                        w = gev_b + oc - 3
                        if w > 0:
                            t.wait_ge(gev.h, w)
                        base = (oc % 2) * 128 * 128
                        for ic in range(128):
                            ins = t.matmul(
                                ps[oc % 4][:, 0:TQ],
                                wbuf[:, base + ic * 128: base + ic * 128 + 128],
                                dap(arena, ic * TQ, [[32768, 128], [1, TQ]]),
                                start=(ic == 0), stop=(ic == 127))
                        gmm.inc(ins, 128)

                @block.vector
                def _(v, tq=tq, gmm_b=gmm_b, x1ql_b=x1ql_b, sto_b=sto_b):
                    v.wait_ge(x1ql.h, x1ql_b + 16)
                    if tq > 0:
                        v.wait_ge(sto.h, sto_b)
                    for oc in range(NCH):
                        v.wait_ge(gmm.h, gmm_b + 128 * (oc + 1))
                        ins = v.tensor_add(zq[:, oc * TQ:(oc + 1) * TQ],
                                           ps[oc % 4][:, 0:TQ],
                                           x1q[:, oc * TQ:(oc + 1) * TQ])
                        gev.inc(ins, 1)

                bank_g = (prev_app,) if prev_app is not None else ()
                app = emit_ln(ctx, block, tag="ln", x_sb=zq, T=TQ,
                              n_ch=NCH, psum4=ps[4:8], onescol32=onescol32,
                              onescol_bf=onescol_bf, onesr32=onesr_sb,
                              sq_bf=sqs, rows=rows, bcast=bcast,
                              eps_ap=epsr[0:1, 0:1], t_tile=TQ,
                              pre_guards=(gev,),
                              bank_guards=(bank_g + ((zev, ini) if tq == 0 else ())))
                prev_app = app

                @block.sync
                def _(s_, tq=tq, app=app):
                    app.wait(s_)
                    dst = dap(x2_o, tq * TQ,
                              [[TPC, 128], [128 * TPC, NCH], [1, TQ]])
                    sto.inc(s_.dma_start(
                        out=dst, in_=zq[:].rearrange("p (c t) -> p c t", c=NCH)),
                        16)

            @block.sync
            def _(s_):
                sto.wait(s_)
    return nc


def run_L0de(x1Ts, inputs):
    nc = build_L0de()
    onesr = np.ones((1, 128), np.float32)
    maker = np.ascontiguousarray(np.asarray(inputs["L0_ffn_rail_maker"], np.float32))
    upt = np.ascontiguousarray(
        np.asarray(inputs["L0_ffn_rail_up_proj"], np.float32).transpose(0, 1, 3, 2))
    cpT = np.ascontiguousarray(np.asarray(inputs["L0_ffn_w_cproj"], np.float32).T)
    in_maps = []
    for c in range(NCORE):
        in_maps.append({"x1_i": x1Ts[c], "maker": maker, "upt": upt,
                        "cpT": cpT, "onesr": onesr})
    res = run_bass_kernel_spmd(nc, in_maps, core_ids=list(range(NCORE)))
    return res.results


def build_L1a():
    ctx = LaunchCtx("L1a")
    nc = ctx.nc
    xT_i = ctx.din("xT_i", [D, TPC])
    wq = ctx.din("wq", [D, D])
    wk = ctx.din("wk", [D, D])
    wv = ctx.din("wv", [D, D])
    qT_o = ctx.dout("qT", [D, TPC])
    kT_o = ctx.dout("kT", [D, TPC])
    vT_o = ctx.dout("vT", [D, TPC])
    nrm_o = ctx.dout("nrm", [2 * TPC])

    with ctx.stack:
        xb = ctx.sbuf("xb", [128, NCH * TPC], BF16)
        qT = ctx.sbuf("qT_sb", [128, NCH * TPC], BF16)
        kT = ctx.sbuf("kT_sb", [128, NCH * TPC], BF16)
        vT = ctx.sbuf("vT_sb", [128, NCH * TPC], BF16)
        sq = ctx.sbuf("sq", [128, NCH * TPC], BF16)
        onescol = ctx.sbuf("onescol", [128, 1], BF16)
        nrow = ctx.sbuf("nrow", [1, 2 * TPC], F32)
        ps = [ctx.psum("ps%d" % i) for i in range(8)]

        with nc.Block() as block:
            xld = ctx.sem("xld")
            ini = ctx.sem("ini")

            @block.gpsimd
            def _(g):
                ini.inc(g.memset(onescol[:], 1.0), 1)
                src = dap(xT_i, 0, [[TPC, 128], [128 * TPC, NCH], [1, TPC]])
                xld.inc(g.dma_start(
                    out=xb[:].rearrange("p (c t) -> p c t", c=NCH), in_=src), 16)

            def mk_evac(tag, dst_sb, scale):
                mmS = ctx.sem(tag + "_mm")
                evS = ctx.sem(tag + "_ev")

                def evac(oc, tt, ps_ap, j):
                    @block.scalar
                    def _(sc):
                        sc.wait_ge(mmS.h, j + 1)
                        ins = sc.activation(
                            dst_sb[:, oc * TPC + tt * 512: oc * TPC + tt * 512 + 512],
                            ps_ap, AF.Copy, scale=scale)
                        evS.inc(ins, 1)
                return evac

            emit_gemm(ctx, block, tag="gq", xb_sb=xb, xb_T=TPC, w_dram=wq,
                      w_cols=D, n_in_ch=NCH, n_out_ch=NCH, psum=ps[0:4],
                      evac=mk_evac("gq", qT, 1.0 / 32.0), tensor_guards=(xld,))
            emit_gemm(ctx, block, tag="gk", xb_sb=xb, xb_T=TPC, w_dram=wk,
                      w_cols=D, n_in_ch=NCH, n_out_ch=NCH, psum=ps[4:8],
                      evac=mk_evac("gk", kT, 1.0))
            emit_gemm(ctx, block, tag="gv", xb_sb=xb, xb_T=TPC, w_dram=wv,
                      w_cols=D, n_in_ch=NCH, n_out_ch=NCH, psum=ps[0:4],
                      evac=mk_evac("gv", vT, 1.0),
                      tensor_guards=(ctx.sems["gq_ev"],))

            nq_ev = emit_norm2(ctx, block, tag="nq", src_bf=qT, T=TPC,
                               n_ch=NCH, sq_bf=sq, onescol_bf=onescol,
                               psum=ps[4:6], out_row_sb=nrow, out_off=0,
                               pre_guards=(ctx.sems["gq_ev"],),
                               bank_guards=(ctx.sems["gk_ev"], xld, ini))
            nk_ev = emit_norm2(ctx, block, tag="nk", src_bf=kT, T=TPC,
                               n_ch=NCH, sq_bf=sq, onescol_bf=onescol,
                               psum=ps[6:8], out_row_sb=nrow, out_off=TPC,
                               pre_guards=(ctx.sems["gk_ev"],
                                           ctx.sems["nq_mm"]))

            st = ctx.sem("st")

            @block.gpsimd
            def _(g):
                for tag2, src_sb, dst in (("gq", qT, qT_o), ("gk", kT, kT_o),
                                          ("gv", vT, vT_o)):
                    ctx.sems[tag2 + "_ev"].wait(g)
                    dstap = dap(dst, 0, [[TPC, 128], [128 * TPC, NCH], [1, TPC]])
                    st.inc(g.dma_start(
                        out=dstap,
                        in_=src_sb[:].rearrange("p (c t) -> p c t", c=NCH)), 16)
                nq_ev.wait(g)
                nk_ev.wait(g)
                st.inc(g.dma_start(
                    out=dap(nrm_o, 0, [[2 * TPC, 1], [1, 2 * TPC]]),
                    in_=nrow[0:1, :]), 16)
                st.wait(g)
    return nc


def run_L1a(xTs, inputs):
    nc = build_L1a()
    in_maps = []
    for c in range(NCORE):
        in_maps.append({
            "xT_i": xTs[c],
            "wq": np.ascontiguousarray(np.asarray(inputs["L1_attn_Wq"], np.float32).T),
            "wk": np.ascontiguousarray(np.asarray(inputs["L1_attn_Wk"], np.float32).T),
            "wv": np.ascontiguousarray(np.asarray(inputs["L1_attn_Wv"], np.float32).T),
        })
    res = run_bass_kernel_spmd(nc, in_maps, core_ids=list(range(NCORE)))
    return res.results


L1B_DO_U = True
L1B_DO_VSUM = True
L1_LPAD = 4736
L1_L2 = 4864


def build_L1b():
    ctx = LaunchCtx("L1b")
    nc = ctx.nc
    qT_i = ctx.din("qT_i", [D, TPC])
    kT_i = ctx.din("kT_i", [D, S])
    v_i = ctx.din("v_i", [S, D])
    nq_i = ctx.din("nq_i", [TPC])
    nk_i = ctx.din("nk_i", [S])
    tblpad = ctx.din("tblpad", [16, L1_LPAD])
    mask_i = ctx.din("mask_i", [16, L1_LPAD])
    c_i = ctx.din("c_i", [16])
    onesr = ctx.din("onesr", [1, 128])
    aT_o = ctx.dout("aT", [D, TPC])

    NKC = S // 128
    NQT = TPC // 512
    u2 = ctx.dint("u2", [16 * 128 * L1_L2], BF16)

    with ctx.stack:
        qb = ctx.sbuf("qb", [128, NCH * TPC], BF16)
        Eb = ctx.sbuf("Eb", [128, NKC * 512], BF16)
        kbuf = ctx.sbuf("kbuf", [128, S], BF16)
        vbuf = ctx.sbuf("vbuf", [128, 2 * NKC * 65], BF16)
        Wt = ctx.sbuf("Wt", [128, L1_L2], BF16)
        zrow = ctx.sbuf("zrow", [128, 1216], BF16)
        csb = ctx.sbuf("csb", [16, 2], F32)
        crow = ctx.sbuf("crow", [1, 16], F32)
        cexp = ctx.sbuf("cexp", [64, 16], F32)
        vsum = ctx.sbuf("vsum", [64, 16], F32)
        cvs = ctx.sbuf("cvs", [64, 16], F32)
        onescol = ctx.sbuf("onescol", [128, 1], BF16)
        onesr_sb = ctx.sbuf("onesr_sb", [1, 128], F32)
        nrows = ctx.sbuf("nrows", [1, TPC + S], F32)
        mrow = ctx.sbuf("mrow", [1, 8], F32)
        mcol = ctx.sbuf("mcol", [128, 1], F32)
        rrow = ctx.sbuf("rrow", [1, 512], F32)
        rb = ctx.sbuf("rb", [64, 512], F32)
        urow = ctx.sbuf("urow", [1, L1_LPAD], F32)
        aT_stg = ctx.sbuf("aT_stg", [64, 2 * 512], F32)
        ps = [ctx.psum("ps%d" % i) for i in range(8)]

        def usb_ap(r0, nr, col0, ncol):
            return Eb[r0:r0 + nr, 2 * col0: 2 * (col0 + ncol)].bitcast(F32)

        with nc.Block() as block:
            cst = ctx.sem("cst")
            qld = ctx.sem("qld")
            kld = ctx.sem("kld")
            vldp = [ctx.sem("vld0"), ctx.sem("vld1")]
            wtl = ctx.sem("wtl")
            smm = ctx.sem("smm")
            sex = ctx.sem("sex")
            pvm = ctx.sem("pvm")
            umm = ctx.sem("umm")
            vsm = ctx.sem("vsm")
            rcp = ctx.sem("rcp")
            summ = ctx.sem("summ")
            nev = ctx.sem("nev")
            cmb = ctx.sem("cmb")
            stop_ = [ctx.sem("sto0"), ctx.sem("sto1")]

            @block.sync
            def _(s_):
                cst.inc(s_.dma_start(out=onesr_sb[:], in_=onesr.ap()), 16)
                cst.inc(s_.dma_start(out=nrows[0:1, 0:TPC],
                                     in_=dap(nq_i, 0, [[TPC, 1], [1, TPC]])), 16)
                cst.inc(s_.dma_start(out=nrows[0:1, TPC:TPC + S],
                                     in_=dap(nk_i, 0, [[S, 1], [1, S]])), 16)
                cst.inc(s_.dma_start(out=usb_ap(0, 16, 0, L1_LPAD),
                                     in_=tblpad.ap()), 16)
                cst.inc(s_.dma_start(out=csb[:, 0:1],
                                     in_=dap(c_i, 0, [[1, 16], [1, 1]])), 16)
                cst.inc(s_.dma_start(out=crow[0:1, 0:16],
                                     in_=dap(c_i, 0, [[16, 1], [1, 16]])), 16)
                cst.inc(s_.dma_start(
                    out=Eb[0:16, 16384:16384 + 2 * L1_LPAD].bitcast(F32),
                    in_=mask_i.ap()), 16)

            ini = ctx.sem("ini")

            @block.gpsimd
            def _(g):
                ini.inc(g.memset(onescol[:], 1.0), 1)
                ini.inc(g.memset(zrow[:], 0.0), 1)
                src = dap(qT_i, 0, [[TPC, 128], [128 * TPC, NCH], [1, TPC]])
                qld.inc(g.dma_start(
                    out=qb[:].rearrange("p (c t) -> p c t", c=NCH), in_=src), 16)

            @block.vector
            def _(v):
                cst.wait(v)
                v.reduce_max(mrow[0:1, 0:1], nrows[0:1, 0:TPC], axis=AX.X)
                ins = v.reduce_max(mrow[0:1, 1:2], nrows[0:1, TPC:TPC + S],
                                   axis=AX.X)
                cst.inc(ins, 1)

            @block.scalar
            def _(sc):
                cst.wait(sc)
                sc.activation(mrow[0:1, 2:4], mrow[0:1, 0:2], AF.Sqrt)
                ins = sc.activation(csb[:, 1:2], csb[:, 0:1], AF.Copy,
                                    scale=-1.0)
                cst.inc(ins, 1)

            @block.vector
            def _(v):
                cst.wait(v)
                v.tensor_mul(mrow[0:1, 4:5], mrow[0:1, 2:3], mrow[0:1, 3:4])
                ins = v.scalar_tensor_tensor(
                    usb_ap(0, 16, 0, L1_LPAD),
                    Eb[0:16, 16384:16384 + 2 * L1_LPAD].bitcast(F32),
                    csb[:, 1:2], usb_ap(0, 16, 0, L1_LPAD),
                    op0=ALU.mult, op1=ALU.add)
                cst.inc(ins, 1)

            @block.scalar
            def _(sc):
                cst.wait(sc)
                ins = sc.activation(mrow[0:1, 5:6], mrow[0:1, 4:5], AF.Copy,
                                    scale=-1.0)
                cst.inc(ins, 1)

            @block.tensor
            def _(t):
                cst.wait(t)
                t.matmul(ps[7][:, 0:1], onesr_sb[0:1, 0:128],
                         mrow[0:1, 5:6], start=True, stop=True)
                ins = t.matmul(ps[6][0:64, 0:16], onesr_sb[0:1, 0:64],
                               crow[0:1, 0:16], start=True, stop=True)
                cst.inc(ins, 1)

            @block.vector
            def _(v):
                cst.wait(v)
                v.tensor_copy(mcol[:, 0:1], ps[7][:, 0:1])
                ins = v.tensor_copy(cexp[0:64, 0:16], ps[6][0:64, 0:16])
                cst.inc(ins, 1)

            ubm = ctx.sem("ubm")
            ube = ctx.sem("ube")
            NUB = (L1_LPAD + 511) // 512

            zfl = ctx.sem("zfl")

            @block.gpsimd
            def _(g):
                cst.wait(g)
                ini.wait(g)
                for h in range(16):
                    for z4 in range(4):
                        zfl.inc(g.dma_start(
                            out=dap(u2, h * 128 * L1_L2 + z4 * 1216,
                                    [[L1_L2, 128], [1, 1216]]),
                            in_=zrow[:, :]), 16)

            url = ctx.sem("url")

            @block.sync
            def _(s_):
                cst.wait(s_)
                for h in range(16):
                    if h > 0:
                        s_.wait_ge(ubm.h, h * NUB)
                    url.inc(s_.dma_start(out=urow[0:1, :],
                                         in_=usb_ap(h, 1, 0, L1_LPAD)), 16)
                    s_.wait_ge(ube.h, (h + 1) * NUB)
                    if h == 0:
                        s_.wait_ge(zfl.h, 16 * 64)
                    dst = dap(u2, h * 128 * L1_L2, [[L1_L2 + 1, 128], [1, L1_LPAD]])
                    wtl.inc(s_.dma_start(out=dst, in_=Wt[:, 0:L1_LPAD]), 16)

            @block.tensor
            def _(t):
                cst.wait(t)
                for h in range(16):
                    t.wait_ge(url.h, 16 * (h + 1))
                    for j in range(NUB):
                        w = 512 if (j + 1) * 512 <= L1_LPAD else L1_LPAD - j * 512
                        gj = h * NUB + j
                        if gj >= 2:
                            t.wait_ge(ube.h, gj - 1)
                        ins = t.matmul(ps[gj % 2][:, 0:w], onesr_sb[0:1, 0:128],
                                       urow[0:1, j * 512:j * 512 + w],
                                       start=True, stop=True)
                        ubm.inc(ins, 1)

            @block.vector
            def _(v):
                for h in range(16):
                    for j in range(NUB):
                        w = 512 if (j + 1) * 512 <= L1_LPAD else L1_LPAD - j * 512
                        gj = h * NUB + j
                        v.wait_ge(ubm.h, gj + 1)
                        if h > 0 and j == 0:
                            v.wait_ge(wtl.h, 16 * h)
                        ins = v.tensor_copy(Wt[:, j * 512:j * 512 + w],
                                            ps[gj % 2][:, 0:w])
                        ube.inc(ins, 1)

            U2N = 16 * 64 // 16 + 16

            for qt in range(NQT):
                for h in range(16):
                    hp = h // 2
                    hb = 64 * (h % 2)
                    gi = qt * 16 + h

                    if h % 2 == 0:
                        @block.gpsimd
                        def _(g, qt=qt, hp=hp):
                            pi = qt * 8 + hp
                            if pi == 0:
                                cst.wait(g)
                            if pi >= 1:
                                g.wait_ge(smm.h, pi * 2 * NKC)
                            src = dap(kT_i, hp * 128 * S, [[S, 128], [1, S]])
                            dst = dap(kbuf, 0, [[S, 128], [1, S]])
                            kld.inc(g.dma_start(out=dst, in_=src), 16)

                    @block.gpsimd
                    def _(g, h=h, gi=gi):
                        if gi >= 2:
                            g.wait_ge(pvm.h, gi - 1)
                            g.wait_ge(umm.h, gi - 1)
                            g.wait_ge(vsm.h, min(gi - 1, 16))
                        base = (gi % 2) * NKC * 65
                        ini.inc(g.memset(vbuf[:, base:base + NKC * 65], 1.0), 1)
                        g.wait_ge(ini.h, ini.n)
                        src = dap(v_i, h * 64, [[D, 128], [128 * D, NKC], [1, 64]])
                        dst = dap(vbuf, base, [[2 * NKC * 65, 128], [65, NKC],
                                               [1, 64]])
                        vldp[gi % 2].inc(g.dma_start(out=dst, in_=src), 16)

                    @block.sync
                    def _(s_, h=h, gi=gi):
                        if gi >= 1:
                            s_.wait_ge(umm.h, gi)
                        else:
                            s_.wait_ge(wtl.h, 16 * 16)
                        wtl.inc(s_.dma_start(
                            out=Wt[:, :],
                            in_=dap(u2, h * 128 * L1_L2,
                                    [[L1_L2, 128], [1, L1_L2]])), 16)

                    @block.tensor
                    def _(t, qt=qt, h=h, hp=hp, hb=hb, gi=gi):
                        if gi == 0:
                            cst.wait(t)
                            ini.wait(t)
                            t.wait_ge(qld.h, 16)
                            t.wait_ge(ube.h, ube.n)
                        pi = qt * 8 + hp
                        t.wait_ge(kld.h, 16 * (pi + 1))
                        for kc in range(NKC):
                            j = gi * NKC + kc
                            if j >= 2:
                                t.wait_ge(sex.h, j - 1)
                            ins = t.matmul(
                                ps[j % 2][:, 0:512],
                                dap(kbuf, hb * S + kc * 128,
                                    [[S, 64], [1, 128]]),
                                dap(qb, hb * NCH * TPC + hp * TPC + qt * 512,
                                    [[NCH * TPC, 64], [1, 512]]),
                                start=True, stop=True)
                            smm.inc(ins, 1)

                    @block.scalar
                    def _(sc, gi=gi):
                        if gi > 0:
                            sc.wait_ge(pvm.h, gi)
                            sc.wait_ge(umm.h, gi)
                        else:
                            sc.wait_ge(wtl.h, 16 * 16)
                        for kc in range(NKC):
                            j = gi * NKC + kc
                            sc.wait_ge(smm.h, j + 1)
                            ins = sc.activation(
                                Eb[:, kc * 512:(kc + 1) * 512],
                                ps[j % 2][:, 0:512], AF.Exp, bias=mcol[:, 0:1])
                            sex.inc(ins, 1)

                    @block.tensor
                    def _(t, qt=qt, h=h, gi=gi):
                        t.wait_ge(sex.h, (gi + 1) * NKC)
                        t.wait_ge(vldp[gi % 2].h, 16 * (gi // 2 + 1))
                        if gi >= 2:
                            t.wait_ge(nev.h, (gi - 1) * 3)
                        base = (gi % 2) * NKC * 65
                        bankp = ps[2 + gi % 2]
                        for kc in range(NKC):
                            ins = t.matmul(
                                bankp[0:65, 0:512],
                                vbuf[:, base + kc * 65: base + kc * 65 + 65],
                                Eb[:, kc * 512:(kc + 1) * 512],
                                start=(kc == 0), stop=(kc == NKC - 1))
                        pvm.inc(ins, 1)
                        if L1B_DO_U:
                            banku = ps[4 + gi % 2]
                            t.wait_ge(wtl.h, 16 * 16 + 16 * (gi + 1))
                            kc0 = max(0, (qt * 512 + 3970) // 128)
                            first = True
                            for kc in range(kc0, NKC):
                                j0 = qt * 512 - kc * 128 + 8191
                                ins = t.matmul(
                                    banku[0:64, 0:512],
                                    vbuf[:, base + kc * 65: base + kc * 65 + 64],
                                    Wt[:, j0:j0 + 512],
                                    start=first, stop=(kc == NKC - 1))
                                first = False
                            umm.inc(ins, 1)
                        else:
                            ins = t.matmul(
                                ps[4 + gi % 2][0:64, 0:512],
                                vbuf[:, base: base + 64],
                                Eb[:, 0:512], start=True, stop=True)
                            umm.inc(ins, 1)
                        if qt == 0 and L1B_DO_VSUM:
                            if h > 0:
                                t.wait_ge(nev.h, 3 * gi)
                            for kc in range(NKC):
                                ins = t.matmul(
                                    ps[7][0:64, h:h + 1],
                                    vbuf[:, base + kc * 65: base + kc * 65 + 64],
                                    onescol[:, 0:1],
                                    start=(kc == 0), stop=(kc == NKC - 1))
                            vsm.inc(ins, 1)
                        elif qt == 0:
                            ins = t.matmul(
                                ps[7][0:64, h:h + 1],
                                vbuf[:, base: base + 64],
                                onescol[:, 0:1], start=True, stop=True)
                            vsm.inc(ins, 1)

                    @block.vector
                    def _(v, gi=gi):
                        v.wait_ge(pvm.h, gi + 1)
                        ins = v.reciprocal(rrow[0:1, 0:512],
                                           ps[2 + gi % 2][64:65, 0:512])
                        rcp.inc(ins, 1)

                    @block.tensor
                    def _(t, gi=gi):
                        t.wait_ge(rcp.h, 2 * gi + 1)
                        ins = t.matmul(ps[6][0:64, 0:512],
                                       onesr_sb[0:1, 0:64],
                                       rrow[0:1, 0:512], start=True, stop=True)
                        summ.inc(ins, 1)

                    @block.vector
                    def _(v, qt=qt, h=h, gi=gi):
                        v.wait_ge(summ.h, gi + 1)
                        ins = v.tensor_copy(rb[0:64, 0:512], ps[6][0:64, 0:512])
                        rcp.inc(ins, 1)
                        v.wait_ge(rcp.h, rcp.n)
                        if qt == 0:
                            v.wait_ge(vsm.h, h + 1)
                            cmb.inc(v.tensor_copy(vsum[0:64, h:h + 1],
                                                  ps[7][0:64, h:h + 1]), 1)
                            v.wait_ge(cmb.h, cmb.n)
                            v.tensor_mul(cvs[0:64, h:h + 1], cexp[0:64, h:h + 1],
                                         vsum[0:64, h:h + 1])
                        v.wait_ge(umm.h, gi + 1)
                        if gi >= 2:
                            v.wait_ge(stop_[gi % 2].h, 16 * (gi // 2))
                        dst = aT_stg[0:64, (gi % 2) * 512:(gi % 2) * 512 + 512]
                        cmb.inc(v.tensor_mul(dst, ps[2 + gi % 2][0:64, 0:512],
                                             rb[0:64, 0:512]), 1)
                        v.wait_ge(cmb.h, cmb.n)
                        cmb.inc(v.tensor_add(dst, dst, ps[4 + gi % 2][0:64, 0:512]), 1)
                        v.wait_ge(cmb.h, cmb.n)
                        ins = v.tensor_scalar_add(dst, dst, cvs[0:64, h:h + 1])
                        nev.inc(ins, 3)

                    @block.sync
                    def _(s_, qt=qt, h=h, gi=gi):
                        s_.wait_ge(nev.h, 3 * (gi + 1))
                        dst = dap(aT_o, h * 64 * TPC + qt * 512,
                                  [[TPC, 64], [1, 512]])
                        stop_[gi % 2].inc(s_.dma_start(
                            out=dst,
                            in_=aT_stg[0:64, (gi % 2) * 512:(gi % 2) * 512 + 512]),
                            16)

            @block.sync
            def _(s_):
                stop_[0].wait(s_)
                stop_[1].wait(s_)
    return nc


def run_L1b(qTs, kT_fulls, v_fulls, nqs, nk_fulls, inputs):
    nc = build_L1b()
    onesr = np.ones((1, 128), np.float32)
    tblT = np.asarray(inputs["L1_attn_rel_bias_table"], np.float32).T
    c_vec = np.ascontiguousarray(tblT[:, 4094])
    in_maps = []
    for c in range(NCORE):
        b, r = c // 4, c % 4
        tp = np.zeros((16, L1_LPAD), np.float32)
        mk = np.zeros((16, L1_LPAD), np.float32)
        lo = r * 2048
        n = max(0, min(L1_LPAD, 4095 - lo))
        if n > 0:
            tp[:, :n] = tblT[:, lo:lo + n]
            mk[:, :n] = 1.0
        in_maps.append({
            "qT_i": qTs[c], "kT_i": kT_fulls[b], "v_i": v_fulls[b],
            "nq_i": nqs[c], "nk_i": nk_fulls[b],
            "tblpad": tp, "mask_i": mk, "c_i": c_vec, "onesr": onesr,
        })
    res = run_bass_kernel_spmd(nc, in_maps, core_ids=list(range(NCORE)))
    return res.results


def build_L1c():
    ctx = LaunchCtx("L1c")
    nc = ctx.nc
    aT_i = ctx.din("aT_i", [D, TPC])
    xT_i = ctx.din("xT_i", [D, TPC])
    gb_i = ctx.din("gb_i", [2, D])
    onesr = ctx.din("onesr", [1, 128])
    x3_o = ctx.dout("x3T", [D, TPC])

    with ctx.stack:
        a = ctx.sbuf("a", [128, NCH * TPC], F32)
        z = ctx.sbuf("z", [128, NCH * TPC], F32)
        sqs = ctx.sbuf("sqs", [128, NCH * TPC], BF16)
        gb = ctx.sbuf("gb", [128, 2 * NCH], F32)
        onescol32 = ctx.sbuf("onescol32", [128, 1], F32)
        onescol_bf = ctx.sbuf("onescol_bf", [128, 1], BF16)
        onesr_sb = ctx.sbuf("onesr_sb", [1, 128], F32)
        rows = ctx.sbuf("rows", [1, 4 * TPC], F32)
        bcast = ctx.sbuf("bcast", [128, 2 * 512], F32)
        epsr = ctx.sbuf("epsr", [1, 1], F32)
        ps = [ctx.psum("ps%d" % i) for i in range(8)]

        with nc.Block() as block:
            ald = ctx.sem("ald")
            xld = ctx.sem("xld")
            gld = ctx.sem("gld")
            orl = ctx.sem("orl")
            ini = ctx.sem("ini")

            @block.gpsimd
            def _(g):
                ini.inc(g.memset(onescol32[:], 1.0), 1)
                ini.inc(g.memset(onescol_bf[:], 1.0), 1)
                ini.inc(g.memset(epsr[:], 1e-5), 1)
                for j in range(2 * NCH):
                    gld.inc(g.dma_start(out=gb[:, j:j + 1],
                                        in_=dap(gb_i, j * 128, [[1, 128], [1, 1]])), 16)

            @block.sync
            def _(s_):
                orl.inc(s_.dma_start(out=onesr_sb[:], in_=onesr.ap()), 16)
                src = dap(aT_i, 0, [[TPC, 128], [128 * TPC, NCH], [1, TPC]])
                ald.inc(s_.dma_start(
                    out=a[:].rearrange("p (c t) -> p c t", c=NCH), in_=src), 16)
                src2 = dap(xT_i, 0, [[TPC, 128], [128 * TPC, NCH], [1, TPC]])
                xld.inc(s_.dma_start(
                    out=z[:].rearrange("p (c t) -> p c t", c=NCH), in_=src2), 16)

            app1 = emit_ln(ctx, block, tag="ln1", x_sb=a, T=TPC, n_ch=NCH,
                           psum4=ps[0:4], onescol32=onescol32,
                           onescol_bf=onescol_bf, onesr32=onesr_sb,
                           sq_bf=sqs, rows=rows, bcast=bcast,
                           eps_ap=epsr[0:1, 0:1],
                           gamma_beta=(gb, 0, NCH),
                           pre_guards=(ald, gld), bank_guards=(gld, ini, orl))

            addv = ctx.sem("addv")

            @block.vector
            def _(v):
                app1.wait(v)
                xld.wait(v)
                for c in range(NCH):
                    ins = v.tensor_add(z[:, c * TPC:(c + 1) * TPC],
                                       z[:, c * TPC:(c + 1) * TPC],
                                       a[:, c * TPC:(c + 1) * TPC])
                addv.inc(ins, 1)

            app2 = emit_ln(ctx, block, tag="ln2", x_sb=z, T=TPC, n_ch=NCH,
                           psum4=ps[4:8], onescol32=onescol32,
                           onescol_bf=onescol_bf, onesr32=onesr_sb,
                           sq_bf=sqs, rows=rows, bcast=bcast,
                           eps_ap=epsr[0:1, 0:1],
                           pre_guards=(addv,),
                           bank_guards=(app1,))

            sto = ctx.sem("sto")

            @block.sync
            def _(s_):
                app2.wait(s_)
                dst = dap(x3_o, 0, [[TPC, 128], [128 * TPC, NCH], [1, TPC]])
                sto.inc(s_.dma_start(
                    out=dst, in_=z[:].rearrange("p (c t) -> p c t", c=NCH)), 16)
                sto.wait(s_)
    return nc


def run_L1c(aTs, xTs, inputs):
    nc = build_L1c()
    onesr = np.ones((1, 128), np.float32)
    gb = np.stack([np.asarray(inputs["L1_attn_ln_gamma"], np.float32),
                   np.asarray(inputs["L1_attn_ln_beta"], np.float32)])
    in_maps = [{"aT_i": aTs[c], "xT_i": xTs[c], "gb_i": gb, "onesr": onesr}
               for c in range(NCORE)]
    res = run_bass_kernel_spmd(nc, in_maps, core_ids=list(range(NCORE)))
    return res.results


def emit_gemm2(ctx, block, *, tag, src_ap_fn, N, w_dram, w_cols, w_row0,
               w_col0, n_ic, n_oc, psum, evac, tensor_guards=(), load_queue="gpsimd"):
    lds = [ctx.sem(tag + "_wld0"), ctx.sem(tag + "_wld1")]
    mm = ctx.sem(tag + "_mm")
    ev = ctx.sem(tag + "_ev")
    if tag + "_wbuf" not in getattr(ctx, "_bufs", {}):
        if not hasattr(ctx, "_bufs"):
            ctx._bufs = {}
        ctx._bufs[tag + "_wbuf"] = ctx.sbuf(tag + "_wbuf",
                                            [128, 2 * n_ic * 128], BF16)
    wbuf = ctx._bufs[tag + "_wbuf"]
    pitch_w = wbuf[:].ap[0][0]
    nb = len(psum)
    ld_b = [lds[0].n, lds[1].n]
    mm_b = mm.n
    ev_b = ev.n

    @block.gpsimd
    def _(g):
        for oc in range(n_oc):
            if oc >= 2:
                g.wait_ge(mm.h, mm_b + oc - 1)
            elif mm_b > 0:
                g.wait_ge(mm.h, mm_b - 1 + oc)
            src = dap(w_dram, w_row0 * w_cols + w_col0 + oc * 128,
                      [[w_cols, 128], [128 * w_cols, n_ic], [1, 128]])
            dst = dap(wbuf, (oc % 2) * n_ic * 128,
                      [[pitch_w, 128], [128, n_ic], [1, 128]])
            lds[oc % 2].inc(g.dma_start(out=dst, in_=src), 16)

    @block.tensor
    def _(t):
        for s in tensor_guards:
            s.wait(t)
        for oc in range(n_oc):
            t.wait_ge(lds[oc % 2].h, ld_b[oc % 2] + 16 * (oc // 2 + 1))
            w = ev_b + oc - nb + 1
            if w > 0:
                t.wait_ge(ev.h, w)
            base = (oc % 2) * n_ic * 128
            for ic in range(n_ic):
                ins = t.matmul(
                    psum[oc % nb][:, 0:N],
                    wbuf[:, base + ic * 128: base + ic * 128 + 128],
                    src_ap_fn(ic),
                    start=(ic == 0), stop=(ic == n_ic - 1))
            mm.inc(ins, 1)

    for oc in range(n_oc):
        evac(oc, psum[oc % nb][:, 0:N], mm_b + oc + 1, ev)
    return mm, ev



def build_L1d():
    ctx = LaunchCtx("L1d")
    nc = ctx.nc
    x_i = ctx.din("x_i", [D, TPC])
    w1 = ctx.din("w1", [D, 4096])
    w2 = ctx.din("w2", [D, 4096])
    h1 = ctx.din("h1", [4096, 4096])
    h2 = ctx.din("h2", [4096, 4096])
    w3 = ctx.din("w3", [4096, D])
    scl_i = ctx.din("scl_i", [5 * 4096])
    onesr = ctx.din("onesr", [1, 128])
    x4_o = ctx.dout("x4T", [D, TPC])

    TH = 512
    NTH = TPC // TH
    NH = 32

    with ctx.stack:
        xb = ctx.sbuf("xb", [128, NCH * TH], BF16)
        xf = ctx.sbuf("xf", [128, NCH * TH], F32)
        abuf = ctx.sbuf("abuf", [128, NH * TH], BF16)
        bbuf = ctx.sbuf("bbuf", [128, NH * TH], BF16)
        zq = ctx.sbuf("zq", [128, NCH * TH], F32)
        sqs = ctx.sbuf("sqs", [128, NCH * TH], BF16)
        scl = ctx.sbuf("scl", [128, 160], F32)
        onescol32 = ctx.sbuf("onescol32", [128, 1], F32)
        onescol_bf = ctx.sbuf("onescol_bf", [128, 1], BF16)
        onesr_sb = ctx.sbuf("onesr_sb", [1, 128], F32)
        rows = ctx.sbuf("rows", [1, 4 * TH], F32)
        bcast = ctx.sbuf("bcast", [128, 2 * 512], F32)
        epsr = ctx.sbuf("epsr", [1, 1], F32)
        ps = [ctx.psum("ps%d" % i) for i in range(8)]

        with nc.Block() as block:
            ini = ctx.sem("ini")
            sld = ctx.sem("sld")
            orl = ctx.sem("orl")
            xld = ctx.sem("xld")
            xfl = ctx.sem("xfl")
            sto = ctx.sem("sto")

            @block.gpsimd
            def _(g):
                ini.inc(g.memset(onescol32[:], 1.0), 1)
                ini.inc(g.memset(onescol_bf[:], 1.0), 1)
                ini.inc(g.memset(epsr[:], 1e-5), 1)
                for j in range(160):
                    sld.inc(g.dma_start(out=scl[:, j:j + 1],
                                        in_=dap(scl_i, j * 128, [[1, 128], [1, 1]])), 16)

            @block.sync
            def _(s_):
                orl.inc(s_.dma_start(out=onesr_sb[:], in_=onesr.ap()), 16)

            def scol(group, oc):
                return scl[:, group * 32 + oc: group * 32 + oc + 1]

            prev_app = None
            for th in range(NTH):
                xld_b, xfl_b, sto_b = xld.n, xfl.n, sto.n

                @block.gpsimd
                def _(g, th=th, xld_b=xld_b):
                    if th > 0:
                        ctx.sems["sA_mm"].wait(g)
                        ctx.sems["sB_mm"].wait(g)
                    src = dap(x_i, th * TH, [[TPC, 128], [128 * TPC, NCH], [1, TH]])
                    xld.inc(g.dma_start(
                        out=xb[:].rearrange("p (c t) -> p c t", c=NCH), in_=src), 16)

                @block.sync
                def _(s_, th=th, xfl_b=xfl_b):
                    if th > 0:
                        ctx.sems["sE_ev"].wait(s_)
                    src = dap(x_i, th * TH, [[TPC, 128], [128 * TPC, NCH], [1, TH]])
                    xfl.inc(s_.dma_start(
                        out=xf[:].rearrange("p (c t) -> p c t", c=NCH), in_=src), 16)

                def xb_ap(ic):
                    return xb[:, ic * TH:(ic + 1) * TH]

                def a_ap(ic):
                    return abuf[:, ic * TH:(ic + 1) * TH]

                def b_ap(ic):
                    return bbuf[:, ic * TH:(ic + 1) * TH]

                def evA(oc, ps_ap, mmt, ev):
                    @block.scalar
                    def _(sc, oc=oc, ps_ap=ps_ap, mmt=mmt, ev=ev):
                        sc.wait_ge(ctx.sems["sA_mm"].h, mmt)
                        if oc == 0:
                            sld.wait(sc)
                        ins = sc.activation(abuf[:, oc * TH:(oc + 1) * TH],
                                            ps_ap, AF.Silu, scale=scol(0, oc))
                        ev.inc(ins, 1)

                gA = (xld, ini) if th == 0 else (xld, ctx.sems["sE_ev"])
                emit_gemm2(ctx, block, tag="sA", src_ap_fn=xb_ap, N=TH,
                           w_dram=w1, w_cols=4096, w_row0=0, w_col0=0,
                           n_ic=NCH, n_oc=NH, psum=ps[0:4], evac=evA,
                           tensor_guards=gA)

                def evB(oc, ps_ap, mmt, ev):
                    @block.vector
                    def _(v, oc=oc, ps_ap=ps_ap, mmt=mmt, ev=ev):
                        v.wait_ge(ctx.sems["sB_mm"].h, mmt)
                        v.wait_ge(ctx.sems["sA_ev"].h, ctx.sems["sA_ev"].n)
                        ins = v.scalar_tensor_tensor(
                            abuf[:, oc * TH:(oc + 1) * TH], ps_ap, scol(1, oc),
                            abuf[:, oc * TH:(oc + 1) * TH],
                            op0=ALU.mult, op1=ALU.mult)
                        ev.inc(ins, 1)

                gB = (ctx.sems["sA_ev"],) if th == 0 else (prev_app,)
                emit_gemm2(ctx, block, tag="sB", src_ap_fn=xb_ap, N=TH,
                           w_dram=w2, w_cols=4096, w_row0=0, w_col0=0,
                           n_ic=NCH, n_oc=NH, psum=ps[4:8], evac=evB,
                           tensor_guards=gB)

                def evC(oc, ps_ap, mmt, ev):
                    @block.scalar
                    def _(sc, oc=oc, ps_ap=ps_ap, mmt=mmt, ev=ev):
                        sc.wait_ge(ctx.sems["sC_mm"].h, mmt)
                        ins = sc.activation(bbuf[:, oc * TH:(oc + 1) * TH],
                                            ps_ap, AF.Gelu, scale=scol(2, oc))
                        ev.inc(ins, 1)

                emit_gemm2(ctx, block, tag="sC", src_ap_fn=a_ap, N=TH,
                           w_dram=h1, w_cols=4096, w_row0=0, w_col0=0,
                           n_ic=NH, n_oc=NH, psum=ps[0:4], evac=evC,
                           tensor_guards=(ctx.sems["sB_ev"],))

                def evD(oc, ps_ap, mmt, ev):
                    @block.scalar
                    def _(sc, oc=oc, ps_ap=ps_ap, mmt=mmt, ev=ev):
                        sc.wait_ge(ctx.sems["sD_mm"].h, mmt)
                        ins = sc.activation(abuf[:, oc * TH:(oc + 1) * TH],
                                            ps_ap, AF.Gelu, scale=scol(3, oc))
                        ev.inc(ins, 1)

                emit_gemm2(ctx, block, tag="sD", src_ap_fn=b_ap, N=TH,
                           w_dram=h2, w_cols=4096, w_row0=0, w_col0=0,
                           n_ic=NH, n_oc=NH, psum=ps[4:8], evac=evD,
                           tensor_guards=(ctx.sems["sC_ev"],))

                def evE(oc, ps_ap, mmt, ev, th=th, sto_b=sto_b):
                    @block.vector
                    def _(v, oc=oc, ps_ap=ps_ap, mmt=mmt, ev=ev):
                        v.wait_ge(ctx.sems["sE_mm"].h, mmt)
                        v.wait_ge(xfl.h, xfl.n)
                        if oc == 0 and th > 0:
                            v.wait_ge(sto.h, sto_b)
                        ins = v.scalar_tensor_tensor(
                            zq[:, oc * TH:(oc + 1) * TH], ps_ap, scol(4, oc),
                            xf[:, oc * TH:(oc + 1) * TH],
                            op0=ALU.mult, op1=ALU.add)
                        ev.inc(ins, 1)

                emit_gemm2(ctx, block, tag="sE", src_ap_fn=a_ap, N=TH,
                           w_dram=w3, w_cols=D, w_row0=0, w_col0=0,
                           n_ic=NH, n_oc=NCH, psum=ps[0:4], evac=evE,
                           tensor_guards=(ctx.sems["sD_ev"],))

                bank_g = [ctx.sems["sE_ev"]]
                if prev_app is not None:
                    bank_g.append(prev_app)
                else:
                    bank_g.append(ini)
                bank_g.append(ctx.sems["sD_ev"])
                app = emit_ln(ctx, block, tag="ln", x_sb=zq, T=TH, n_ch=NCH,
                              psum4=ps[4:8], onescol32=onescol32,
                              onescol_bf=onescol_bf, onesr32=onesr_sb,
                              sq_bf=sqs, rows=rows, bcast=bcast,
                              eps_ap=epsr[0:1, 0:1],
                              pre_guards=(ctx.sems["sE_ev"],),
                              bank_guards=tuple(bank_g) + (orl,))
                prev_app = app

                @block.sync
                def _(s_, th=th, app=app):
                    app.wait(s_)
                    dst = dap(x4_o, th * TH, [[TPC, 128], [128 * TPC, NCH], [1, TH]])
                    sto.inc(s_.dma_start(
                        out=dst, in_=zq[:].rearrange("p (c t) -> p c t", c=NCH)), 16)

            @block.sync
            def _(s_):
                sto.wait(s_)
    return nc


def run_L1d(xTs, inputs):
    nc = build_L1d()
    onesr = np.ones((1, 128), np.float32)
    scl = np.zeros(5 * 4096, np.float32)
    scl[0:4096] = np.asarray(inputs["L1_ffn_w1_s"], np.float32)
    scl[4096:8192] = np.asarray(inputs["L1_ffn_w2_s"], np.float32)
    scl[8192:12288] = np.asarray(inputs["L1_ffn_h1_s"], np.float32)
    scl[12288:16384] = np.asarray(inputs["L1_ffn_h2_s"], np.float32)
    scl[16384:17408] = np.asarray(inputs["L1_ffn_w3_s"], np.float32)
    w1 = np.ascontiguousarray(np.asarray(inputs["L1_ffn_w1_w"], np.float32).T)
    w2 = np.ascontiguousarray(np.asarray(inputs["L1_ffn_w2_w"], np.float32).T)
    h1 = np.ascontiguousarray(np.asarray(inputs["L1_ffn_h1_w"], np.float32).T)
    h2 = np.ascontiguousarray(np.asarray(inputs["L1_ffn_h2_w"], np.float32).T)
    w3 = np.ascontiguousarray(np.asarray(inputs["L1_ffn_w3_w"], np.float32).T)
    in_maps = [{"x_i": xTs[c], "w1": w1, "w2": w2, "h1": h1, "h2": h2,
                "w3": w3, "scl_i": scl, "onesr": onesr} for c in range(NCORE)]
    res = run_bass_kernel_spmd(nc, in_maps, core_ids=list(range(NCORE)))
    return res.results


def build_L2a1():
    ctx = LaunchCtx("L2a1")
    nc = ctx.nc
    x_i = ctx.din("x_i", [D, TPC])
    wh = ctx.din("wh", [D, 4096])
    wqk = ctx.din("wqk", [D, 128])
    bias_i = ctx.din("bias_i", [6 * 4096])
    onesr = ctx.din("onesr", [1, 128])
    v_o = ctx.dout("vT", [2048, TPC])
    gate_o = ctx.dout("gateT", [2048, TPC])
    qk_o = ctx.dout("qkT", [128, TPC])

    with ctx.stack:
        xf = ctx.sbuf("xf", [128, NCH * TPC], F32)
        normb = ctx.sbuf("normb", [128, NCH * TPC], BF16)
        sqs = ctx.sbuf("sqs", [128, NCH * TPC], BF16)
        stg = ctx.sbuf("stg", [128, 4 * 512], F32)
        bias = ctx.sbuf("bias", [128, 6 * 32], F32)
        onescol32 = ctx.sbuf("onescol32", [128, 1], F32)
        onescol_bf = ctx.sbuf("onescol_bf", [128, 1], BF16)
        onesr_sb = ctx.sbuf("onesr_sb", [1, 128], F32)
        rows = ctx.sbuf("rows", [1, 4 * TPC], F32)
        bcast = ctx.sbuf("bcast", [128, 2 * 512], F32)
        epsr = ctx.sbuf("epsr", [1, 1], F32)
        ps = [ctx.psum("ps%d" % i) for i in range(8)]

        with nc.Block() as block:
            ini = ctx.sem("ini")
            bld = ctx.sem("bld")
            xld = ctx.sem("xld")
            orl = ctx.sem("orl")
            nrm = ctx.sem("nrm")
            vhp = [ctx.sem("vh%d" % i) for i in range(4)]
            vqp = [ctx.sem("vq%d" % i) for i in range(4)]

            @block.gpsimd
            def _(g):
                ini.inc(g.memset(onescol32[:], 1.0), 1)
                ini.inc(g.memset(onescol_bf[:], 1.0), 1)
                ini.inc(g.memset(epsr[:], 1e-5), 1)
                for j in range(4 * 32):
                    bld.inc(g.dma_start(out=bias[:, j:j + 1],
                                        in_=dap(bias_i, j * 128, [[1, 128], [1, 1]])), 16)

            @block.sync
            def _(s_):
                orl.inc(s_.dma_start(out=onesr_sb[:], in_=onesr.ap()), 16)
                src = dap(x_i, 0, [[TPC, 128], [128 * TPC, NCH], [1, TPC]])
                xld.inc(s_.dma_start(
                    out=xf[:].rearrange("p (c t) -> p c t", c=NCH), in_=src), 16)

            app0 = emit_ln(ctx, block, tag="ln0", x_sb=xf, T=TPC, n_ch=NCH,
                           psum4=ps[0:4], onescol32=onescol32,
                           onescol_bf=onescol_bf, onesr32=onesr_sb,
                           sq_bf=sqs, rows=rows, bcast=bcast,
                           eps_ap=epsr[0:1, 0:1],
                           gamma_beta=(bias, 64, 96),
                           pre_guards=(xld, bld), bank_guards=(ini, orl))

            @block.scalar
            def _(sc):
                app0.wait(sc)
                for c in range(NCH):
                    ins = sc.activation(normb[:, c * TPC:(c + 1) * TPC],
                                        xf[:, c * TPC:(c + 1) * TPC], AF.Copy)
                    nrm.inc(ins, 1)

            def evH(oc, tt, ps_ap, j):
                @block.scalar
                def _(sc, oc=oc, tt=tt, ps_ap=ps_ap, j=j):
                    sc.wait_ge(ctx.sems["gh_mm"].h, j + 1)
                    sl = (j % 4) * 512
                    if j >= 4:
                        sc.wait_ge(vhp[j % 4].h, 16 * (j // 4))
                    ins = sc.activation(stg[:, sl:sl + 512], ps_ap, AF.Silu,
                                        bias=bias[:, oc:oc + 1])
                    ctx.sems["gh_ev"].inc(ins, 1)

                @block.sync
                def _(s_, oc=oc, tt=tt, j=j):
                    s_.wait_ge(ctx.sems["gh_ev"].h, j + 1)
                    sl = (j % 4) * 512
                    dst_t = v_o if oc < 16 else gate_o
                    r0 = (oc if oc < 16 else oc - 16) * 128
                    vhp[j % 4].inc(s_.dma_start(
                        out=dap(dst_t, r0 * TPC + tt * 512, [[TPC, 128], [1, 512]]),
                        in_=stg[:, sl:sl + 512]), 16)

            emit_gemm(ctx, block, tag="gh", xb_sb=normb, xb_T=TPC,
                      w_dram=wh, w_cols=4096, n_in_ch=NCH, n_out_ch=32,
                      psum=ps[0:4], evac=evH, tensor_guards=(nrm, app0))

            def evQ(oc, tt, ps_ap, j):
                @block.scalar
                def _(sc, tt=tt, ps_ap=ps_ap, j=j):
                    sc.wait_ge(ctx.sems["gq_mm"].h, j + 1)
                    sl = (j % 4) * 512
                    vhp[j % 4].wait(sc)
                    ins = sc.activation(stg[:, sl:sl + 512], ps_ap, AF.Silu,
                                        bias=bias[:, 32:33])
                    ctx.sems["gq_ev"].inc(ins, 1)

                @block.sync
                def _(s_, tt=tt, j=j):
                    s_.wait_ge(ctx.sems["gq_ev"].h, j + 1)
                    sl = (j % 4) * 512
                    vqp[j % 4].inc(s_.dma_start(
                        out=dap(qk_o, tt * 512, [[TPC, 128], [1, 512]]),
                        in_=stg[:, sl:sl + 512]), 16)

            emit_gemm(ctx, block, tag="gq", xb_sb=normb, xb_T=TPC,
                      w_dram=wqk, w_cols=128, n_in_ch=NCH, n_out_ch=1,
                      psum=ps[4:8], evac=evQ,
                      tensor_guards=(ctx.sems["gh_ev"],))

            @block.sync
            def _(s_):
                for sp in vhp + vqp:
                    sp.wait(s_)
    return nc



def build_L2a2():
    ctx = LaunchCtx("L2a2")
    nc = ctx.nc
    v_i = ctx.din("v_i", [2048, TPC])
    qk_i = ctx.din("qk_i", [128, TPC])
    osgb_i = ctx.din("osgb_i", [8 * 128])
    masks_i = ctx.din("masks_i", [32, 2, 128, 256], BF16)
    re_i = ctx.din("re_i", [32])
    onesr = ctx.din("onesr", [1, 128])
    ident = ctx.din("ident", [128, 128])
    quad_o = ctx.dout("quadT", [2048, TPC])
    linq_o = ctx.dout("linqT", [128, TPC])
    lkv_o = ctx.dout("lkvT", [128, TPC])

    NG = TPC // 256
    NVC = 16
    with ctx.stack:
        vfm = ctx.sbuf("vfm", [128, NVC * TPC], BF16)
        vtok = ctx.sbuf("vtok", [128, NVC * TPC], BF16)
        qkb = ctx.sbuf("qkb", [128, TPC], BF16)
        heads = ctx.sbuf("heads", [128, 4 * TPC], BF16)
        lktok = ctx.sbuf("lktok", [128, 16 * 128], BF16)
        bT = ctx.sbuf("bT", [128, 2 * 256], F32)
        mtile = ctx.sbuf("mtile", [128, 2 * 256], BF16)
        rebc = ctx.sbuf("rebc", [128, 32], F32)
        rerow = ctx.sbuf("rerow", [1, 32], F32)
        att = ctx.sbuf("att", [128, 2 * 256], F32)
        attnb = ctx.sbuf("attnb", [128, 2 * 256], BF16)
        stg = ctx.sbuf("stg", [128, 4 * 256], F32)
        stg2 = ctx.sbuf("stg2", [128, 4 * 512], F32)
        osgb = ctx.sbuf("osgb", [128, 8], F32)
        onesr_sb = ctx.sbuf("onesr_sb", [1, 128], F32)
        idsb = ctx.sbuf("idsb", [128, 128], F32)
        idbf = ctx.sbuf("idbf", [128, 128], BF16)
        ps = [ctx.psum("ps%d" % i) for i in range(6)]
        psb = [ctx.psum("psb%d" % i, (128, 1024), BF16) for i in range(2)]

        with nc.Block() as block:
            cst = ctx.sem("cst")
            mldp = [ctx.sem("mld0"), ctx.sem("mld1")]
            tr = ctx.sem("tr")
            tev = ctx.sem("tev")
            hd = ctx.sem("hd")
            smm = ctx.sem("smm")
            aevv = ctx.sem("aevv")
            aevs = ctx.sem("aevs")
            qmm = ctx.sem("qmm")
            qev = ctx.sem("qev")
            lmm = ctx.sem("lmm")
            lev = ctx.sem("lev")
            sto = ctx.sem("sto")
            stq = [ctx.sem("stq%d" % i) for i in range(4)]

            @block.sync
            def _(s_):
                cst.inc(s_.dma_start(out=onesr_sb[:], in_=onesr.ap()), 16)
                cst.inc(s_.dma_start(out=idsb[:], in_=ident.ap()), 16)
                cst.inc(s_.dma_start(out=rerow[0:1, :],
                                     in_=dap(re_i, 0, [[32, 1], [1, 32]])), 16)

            gld = ctx.sem("gld")

            @block.gpsimd
            def _(g):
                for j in range(8):
                    gld.inc(g.dma_start(out=osgb[:, j:j + 1],
                                        in_=dap(osgb_i, j * 128, [[1, 128], [1, 1]])), 16)
                src = dap(v_i, 0, [[TPC, 128], [128 * TPC, NVC], [1, TPC]])
                gld.inc(g.dma_start(
                    out=vfm[:].rearrange("p (c t) -> p c t", c=NVC), in_=src), 16)
                src2 = dap(qk_i, 0, [[TPC, 128], [1, TPC]])
                gld.inc(g.dma_start(out=qkb[:, :], in_=src2), 16)
                gld.inc(g.dma_start(out=idbf[:], in_=ident.ap()), 16)

            @block.tensor
            def _(t):
                cst.wait(t)
                ins = t.matmul(ps[5][:, 0:32], onesr_sb[0:1, 0:128],
                               rerow[0:1, 0:32], start=True, stop=True)
                cst.inc(ins, 1)

            @block.vector
            def _(v):
                cst.wait(v)
                gld.wait(v)
                ins = v.tensor_scalar_mul(rebc[:, 0:32], ps[5][:, 0:32],
                                          float(math.sqrt(128.0)))
                cst.inc(ins, 1)
                for hh in range(4):
                    ins = v.tensor_scalar(heads[:, hh * TPC:(hh + 1) * TPC],
                                          qkb[:, :], osgb[:, hh:hh + 1],
                                          osgb[:, 4 + hh:5 + hh],
                                          op0=ALU.mult, op1=ALU.add)
                    hd.inc(ins, 1)

            lqs = ctx.sem("lqs")

            @block.gpsimd
            def _(g):
                hd.wait(g)
                lqs.inc(g.dma_start(
                    out=dap(linq_o, 0, [[TPC, 128], [1, TPC]]),
                    in_=heads[:, 1 * TPC:2 * TPC]), 16)

            @block.sync
            def _(s_):
                cst.wait(s_)
                for gi in range(64):
                    if gi >= 2:
                        s_.wait_ge(cst.h, cst.n + gi - 1)
                    mldp[gi % 2].inc(s_.dma_start(
                        out=mtile[:, (gi % 2) * 256:(gi % 2) * 256 + 256],
                        in_=dap(masks_i, gi * 128 * 256, [[256, 128], [1, 256]])), 16)

            @block.vector
            def _(v):
                cst.wait(v)
                for b in range(32):
                    for jc in range(2):
                        gi = b * 2 + jc
                        v.wait_ge(mldp[gi % 2].h, 16 * (gi // 2 + 1))
                        dst = bT[:, jc * 256:(jc + 1) * 256]
                        if b == 0:
                            ins = v.tensor_scalar_mul(
                                dst, mtile[:, (gi % 2) * 256:(gi % 2) * 256 + 256],
                                rebc[:, b:b + 1])
                        else:
                            ins = v.scalar_tensor_tensor(
                                dst, mtile[:, (gi % 2) * 256:(gi % 2) * 256 + 256],
                                rebc[:, b:b + 1], dst, op0=ALU.mult, op1=ALU.add)
                        cst.inc(ins, 1)
                        v.wait_ge(cst.h, cst.n)

            @block.tensor
            def _(t):
                cst.wait(t)
                gld.wait(t)
                hd.wait(t)
                for j in range(272):
                    if j >= 2:
                        t.wait_ge(tev.h, j - 1)
                    if j < 256:
                        tc, ec = j // 16, j % 16
                        src = vfm[:, ec * TPC + tc * 128: ec * TPC + tc * 128 + 128]
                    else:
                        tc = j - 256
                        src = heads[:, 3 * TPC + tc * 128: 3 * TPC + tc * 128 + 128]
                    ins = t.transpose(psb[j % 2][:, 0:128], src, idbf[:])
                    tr.inc(ins, 1)

            @block.vector
            def _(v):
                for j in range(272):
                    v.wait_ge(tr.h, j + 1)
                    if j < 256:
                        tc, ec = j // 16, j % 16
                        dst = vtok[:, tc * TPC + ec * 128: tc * TPC + ec * 128 + 128]
                    else:
                        tc = j - 256
                        dst = lktok[:, tc * 128:(tc + 1) * 128]
                    ins = v.tensor_copy(dst, psb[j % 2][:, 0:128])
                    tev.inc(ins, 1)

            for g in range(NG):
                @block.tensor
                def _(t, g=g):
                    if g == 0:
                        tev.wait(t)
                    for jc in range(2):
                        j = g * 2 + jc
                        if j >= 2:
                            t.wait_ge(aevv.h, 2 * j - 1)
                        ins = t.matmul(
                            ps[jc][:, 0:256],
                            heads[:, 2 * TPC + g * 256 + jc * 128:
                                   2 * TPC + g * 256 + jc * 128 + 128],
                            heads[:, 0 * TPC + g * 256: 0 * TPC + g * 256 + 256],
                            start=True, stop=True)
                        smm.inc(ins, 1)

                @block.vector
                def _(v, g=g):
                    if g > 0:
                        v.wait_ge(aevs.h, 2 * g)
                    for jc in range(2):
                        j = g * 2 + jc
                        v.wait_ge(smm.h, j + 1)
                        d1 = att[:, jc * 256:(jc + 1) * 256]
                        aevv.inc(v.scalar_tensor_tensor(
                            d1, ps[jc][:, 0:256], 1.0 / 256.0,
                            bT[:, jc * 256:(jc + 1) * 256],
                            op0=ALU.mult, op1=ALU.add), 1)
                        v.wait_ge(aevv.h, aevv.n)
                        aevv.inc(v.tensor_scalar_max(d1, d1, 0.0), 1)
                        v.wait_ge(aevv.h, aevv.n)

                @block.scalar
                def _(sc, g=g):
                    if g > 0:
                        sc.wait_ge(qmm.h, 16 * g)
                    for jc in range(2):
                        j = g * 2 + jc
                        sc.wait_ge(aevv.h, 2 * j + 2)
                        ins = sc.activation(attnb[:, jc * 256:(jc + 1) * 256],
                                            att[:, jc * 256:(jc + 1) * 256],
                                            AF.Square)
                        aevs.inc(ins, 1)

                @block.tensor
                def _(t, g=g):
                    t.wait_ge(aevs.h, 2 * g + 2)
                    for ec in range(16):
                        j = g * 16 + ec
                        if j >= 2:
                            t.wait_ge(qev.h, j - 1)
                        for jc in range(2):
                            ins = t.matmul(
                                ps[2 + j % 2][:, 0:256],
                                vtok[:, (2 * g + jc) * TPC + ec * 128:
                                     (2 * g + jc) * TPC + ec * 128 + 128],
                                attnb[:, jc * 256:(jc + 1) * 256],
                                start=(jc == 0), stop=(jc == 1))
                        qmm.inc(ins, 1)

                @block.vector
                def _(v, g=g):
                    for ec in range(16):
                        j = g * 16 + ec
                        v.wait_ge(qmm.h, j + 1)
                        if j >= 4:
                            v.wait_ge(stq[j % 4].h, 16 * (j // 4))
                        ins = v.tensor_copy(stg[:, (j % 4) * 256:(j % 4) * 256 + 256],
                                            ps[2 + j % 2][:, 0:256])
                        qev.inc(ins, 1)

                @block.sync
                def _(s_, g=g):
                    for ec in range(16):
                        j = g * 16 + ec
                        s_.wait_ge(qev.h, j + 1)
                        stq[j % 4].inc(s_.dma_start(
                            out=dap(quad_o, ec * 128 * TPC + g * 256,
                                    [[TPC, 128], [1, 256]]),
                            in_=stg[:, (j % 4) * 256:(j % 4) * 256 + 256]), 16)

            @block.tensor
            def _(t):
                for en in range(4):
                    if en >= 1:
                        t.wait_ge(lev.h, en)
                    for tc in range(16):
                        ins = t.matmul(
                            ps[4][:, 0:512],
                            lktok[:, tc * 128:(tc + 1) * 128],
                            vtok[:, tc * TPC + en * 512: tc * TPC + en * 512 + 512],
                            start=(tc == 0), stop=(tc == 15))
                    lmm.inc(ins, 1)

            @block.vector
            def _(v):
                for en in range(4):
                    v.wait_ge(lmm.h, en + 1)
                    ins = v.tensor_copy(stg2[:, en * 512:(en + 1) * 512],
                                        ps[4][:, 0:512])
                    lev.inc(ins, 1)

            @block.sync
            def _(s_):
                lev.wait(s_)
                for sp in stq:
                    sp.wait(s_)
                sto.inc(s_.dma_start(
                    out=dap(lkv_o, 0, [[TPC, 128], [1, TPC]]),
                    in_=stg2[:, 0:TPC]), 16)
                sto.wait(s_)

            @block.gpsimd
            def _(g):
                lqs.wait(g)
    return nc


def build_L2b():
    ctx = LaunchCtx("L2b")
    nc = ctx.nc
    x_i = ctx.din("x_i", [D, TPC])
    quad_i = ctx.din("quad_i", [2048, TPC])
    gate_i = ctx.din("gate_i", [2048, TPC])
    linq_i = ctx.din("linq_i", [128, TPC])
    lkvp_i = ctx.din("lkvp_i", [4, 128, TPC])
    wout = ctx.din("wout", [2048, D])
    bout = ctx.din("bout", [D])
    onesr = ctx.din("onesr", [1, 128])
    x5_o = ctx.dout("x5T", [D, TPC])

    with ctx.stack:
        xf = ctx.sbuf("xf", [128, NCH * TPC], F32)
        goT = ctx.sbuf("goT", [128, 16 * TPC], BF16)
        lkv4 = ctx.sbuf("lkv4", [128, 4 * TPC], F32)
        lkvb = ctx.sbuf("lkvb", [128, TPC], BF16)
        lqb = ctx.sbuf("lqb", [128, TPC], BF16)
        qg = ctx.sbuf("qg", [128, 4 * 512], F32)
        gg = ctx.sbuf("gg", [128, 4 * 512], F32)
        tsc = ctx.sbuf("tsc", [128, 2 * 512], F32)
        bias = ctx.sbuf("bias", [128, NCH], F32)
        onescol32 = ctx.sbuf("onescol32", [128, 1], F32)
        onescol_bf = ctx.sbuf("onescol_bf", [128, 1], BF16)
        onesr_sb = ctx.sbuf("onesr_sb", [1, 128], F32)
        bcast = ctx.sbuf("bcast", [128, 2 * 512], F32)
        epsr = ctx.sbuf("epsr", [1, 1], F32)
        ps = [ctx.psum("ps%d" % i) for i in range(8)]
        sqs = lkv4[:].bitcast(BF16)
        rows = goT[0:1, 0:16384].bitcast(F32)

        with nc.Block() as block:
            ini = ctx.sem("ini")
            bld = ctx.sem("bld")
            xld = ctx.sem("xld")
            lld = ctx.sem("lld")
            orl = ctx.sem("orl")
            lsum = ctx.sem("lsum")
            lmm = ctx.sem("lmm")
            qglp = [ctx.sem("qgl%d" % i) for i in range(4)]
            lev = ctx.sem("lev")
            sto = ctx.sem("sto")

            @block.gpsimd
            def _(g):
                ini.inc(g.memset(onescol32[:], 1.0), 1)
                ini.inc(g.memset(onescol_bf[:], 1.0), 1)
                ini.inc(g.memset(epsr[:], 1e-5), 1)
                for j in range(NCH):
                    bld.inc(g.dma_start(out=bias[:, j:j + 1],
                                        in_=dap(bout, j * 128, [[1, 128], [1, 1]])), 16)
                lql = ctx.sem("lql")
                lql.inc(g.dma_start(out=lqb[:, :],
                                    in_=dap(linq_i, 0, [[TPC, 128], [1, TPC]])), 16)

            @block.sync
            def _(s_):
                orl.inc(s_.dma_start(out=onesr_sb[:], in_=onesr.ap()), 16)
                src = dap(x_i, 0, [[TPC, 128], [128 * TPC, NCH], [1, TPC]])
                xld.inc(s_.dma_start(
                    out=xf[:].rearrange("p (c t) -> p c t", c=NCH), in_=src), 16)
                src2 = dap(lkvp_i, 0, [[TPC, 128], [128 * TPC, 4], [1, TPC]])
                lld.inc(s_.dma_start(
                    out=lkv4[:].rearrange("p (c t) -> p c t", c=4), in_=src2), 16)

            @block.vector
            def _(v):
                lld.wait(v)
                lsum.inc(v.tensor_add(lkv4[:, 0:TPC], lkv4[:, 0:TPC],
                                      lkv4[:, TPC:2 * TPC]), 1)
                lsum.inc(v.tensor_add(lkv4[:, 2 * TPC:3 * TPC],
                                      lkv4[:, 2 * TPC:3 * TPC],
                                      lkv4[:, 3 * TPC:4 * TPC]), 1)
                v.wait_ge(lsum.h, lsum.n)
                lsum.inc(v.tensor_add(lkv4[:, 0:TPC], lkv4[:, 0:TPC],
                                      lkv4[:, 2 * TPC:3 * TPC]), 1)
                v.wait_ge(lsum.h, lsum.n)

            @block.scalar
            def _(sc):
                lsum.wait(sc)
                ins = sc.activation(lkvb[:, :], lkv4[:, 0:TPC], AF.Copy,
                                    scale=1.0 / float(S))
                lsum.inc(ins, 1)

            for ec in range(16):
                @block.sync
                def _(s_, ec=ec):
                    for tt in range(4):
                        j = ec * 4 + tt
                        if j >= 4:
                            s_.wait_ge(lev.h, 2 * (j - 3))
                        qglp[j % 4].inc(s_.dma_start(
                            out=qg[:, (j % 4) * 512:(j % 4) * 512 + 512],
                            in_=dap(quad_i, ec * 128 * TPC + tt * 512,
                                    [[TPC, 128], [1, 512]])), 16)
                        qglp[j % 4].inc(s_.dma_start(
                            out=gg[:, (j % 4) * 512:(j % 4) * 512 + 512],
                            in_=dap(gate_i, ec * 128 * TPC + tt * 512,
                                    [[TPC, 128], [1, 512]])), 16)

                @block.tensor
                def _(t, ec=ec):
                    if ec == 0:
                        lsum.wait(t)
                        ini.wait(t)
                        ctx.sems["lql"].wait(t)
                    for tt in range(4):
                        j = ec * 4 + tt
                        if j >= 8:
                            t.wait_ge(lev.h, 2 * (j - 7))
                        ins = t.matmul(ps[j % 8][:, 0:512],
                                       lkvb[:, ec * 128:(ec + 1) * 128],
                                       lqb[:, tt * 512:(tt + 1) * 512],
                                       start=True, stop=True)
                        lmm.inc(ins, 1)

                @block.vector
                def _(v, ec=ec):
                    for tt in range(4):
                        j = ec * 4 + tt
                        v.wait_ge(lmm.h, j + 1)
                        v.wait_ge(qglp[j % 4].h, 32 * (j // 4 + 1))
                        dst = goT[:, ec * TPC + tt * 512: ec * TPC + tt * 512 + 512]
                        lev.inc(v.tensor_add(dst, ps[j % 8][:, 0:512],
                                             qg[:, (j % 4) * 512:(j % 4) * 512 + 512]), 1)
                        v.wait_ge(lev.h, lev.n)
                        lev.inc(v.tensor_mul(dst, dst,
                                             gg[:, (j % 4) * 512:(j % 4) * 512 + 512]), 1)
                        v.wait_ge(lev.h, lev.n)

            gev = ctx.sem("gw_ev2")

            def evW(oc, tt, ps_ap, j):
                @block.vector
                def _(v, oc=oc, tt=tt, ps_ap=ps_ap, j=j):
                    v.wait_ge(ctx.sems["gw_mm"].h, j + 1)
                    if j == 0:
                        bld.wait(v)
                        xld.wait(v)
                    t1 = tsc[:, (j % 2) * 512:(j % 2) * 512 + 512]
                    if j >= 2:
                        v.wait_ge(ctx.sems["gw_ev"].h, j - 1)
                    gev.inc(v.tensor_scalar_add(t1, ps_ap, bias[:, oc:oc + 1]), 1)
                    v.wait_ge(gev.h, gev.n)
                    xs_ = xf[:, oc * TPC + tt * 512: oc * TPC + tt * 512 + 512]
                    ctx.sems["gw_ev"].inc(
                        v.scalar_tensor_tensor(xs_, xs_, 2.0, t1,
                                               op0=ALU.mult, op1=ALU.add), 1)

            emit_gemm(ctx, block, tag="gw", xb_sb=goT, xb_T=TPC,
                      w_dram=wout, w_cols=D, n_in_ch=16, n_out_ch=NCH,
                      psum=ps[0:4], evac=evW,
                      tensor_guards=(ctx.sems["lev"],))

            app = emit_ln(ctx, block, tag="ln", x_sb=xf, T=TPC, n_ch=NCH,
                          psum4=ps[4:8], onescol32=onescol32,
                          onescol_bf=onescol_bf, onesr32=onesr_sb,
                          sq_bf=sqs, rows=rows, bcast=bcast,
                          eps_ap=epsr[0:1, 0:1],
                          pre_guards=(ctx.sems["gw_ev"],),
                          bank_guards=(ctx.sems["lev"], ini, orl))

            @block.sync
            def _(s_):
                app.wait(s_)
                dst = dap(x5_o, 0, [[TPC, 128], [128 * TPC, NCH], [1, TPC]])
                src = dap(xf, 0, [[NCH * TPC, 128], [TPC, NCH], [1, TPC]])
                sto.inc(s_.dma_start(out=dst, in_=src), 16)
                sto.wait(s_)
    return nc



def build_L2c():
    ctx = LaunchCtx("L2c")
    nc = ctx.nc
    x_i = ctx.din("x_i", [D, TPC])
    bcol_i = ctx.din("bcol_i", [D])
    wr = ctx.din("wr", [D, D])
    wk = ctx.din("wk", [D, D])
    wv = ctx.din("wv", [D, D])
    mus = ctx.din("mus", [4 * D])
    onesr = ctx.din("onesr", [1, 128])
    x6_o = ctx.dout("x6T", [D, TPC])

    CW = NCH * TPC
    with ctx.stack:
        xf = ctx.sbuf("xf", [128, CW], F32)
        arena = ctx.sbuf("arena", [128, 3 * CW], BF16)
        tsc = ctx.sbuf("tsc", [128, 2 * 512], F32)
        mu = ctx.sbuf("mu", [128, 4 * NCH], F32)
        bcol = ctx.sbuf("bcol", [128, NCH], F32)
        onescol32 = ctx.sbuf("onescol32", [128, 1], F32)
        onescol_bf = ctx.sbuf("onescol_bf", [128, 1], BF16)
        onesr_sb = ctx.sbuf("onesr_sb", [1, 128], F32)
        bcast = ctx.sbuf("bcast", [128, 2 * 512], F32)
        epsr = ctx.sbuf("epsr", [1, 1], F32)
        ps = [ctx.psum("ps%d" % i) for i in range(8)]
        rows = None

        def sl(slot, c, t0=0, tn=TPC):
            return arena[:, slot * CW + c * TPC + t0: slot * CW + c * TPC + t0 + tn]

        with nc.Block() as block:
            ini = ctx.sem("ini")
            bld = ctx.sem("bld")
            xld = ctx.sem("xld")
            orl = ctx.sem("orl")
            mix = ctx.sem("mix")

            @block.gpsimd
            def _(g):
                ini.inc(g.memset(onescol32[:], 1.0), 1)
                ini.inc(g.memset(onescol_bf[:], 1.0), 1)
                ini.inc(g.memset(epsr[:], 1e-5), 1)
                for j in range(4 * NCH):
                    bld.inc(g.dma_start(out=mu[:, j:j + 1],
                                        in_=dap(mus, j * 128, [[1, 128], [1, 1]])), 16)
                for j in range(NCH):
                    bld.inc(g.dma_start(out=bcol[:, j:j + 1],
                                        in_=dap(bcol_i, j * 128, [[1, 128], [1, 1]])), 16)

            @block.sync
            def _(s_):
                orl.inc(s_.dma_start(out=onesr_sb[:], in_=onesr.ap()), 16)
                src = dap(x_i, 0, [[TPC, 128], [128 * TPC, NCH], [1, TPC]])
                xld.inc(s_.dma_start(
                    out=xf[:].rearrange("p (c t) -> p c t", c=NCH), in_=src), 16)

            @block.vector
            def _(v):
                xld.wait(v)
                bld.wait(v)
                for slot, m0, m1 in ((0, 0, 1), (1, 2, 3)):
                    for c in range(NCH):
                        mix.inc(v.tensor_scalar_mul(
                            sl(slot, c, 1, TPC - 1),
                            xf[:, c * TPC + 1:(c + 1) * TPC],
                            mu[:, m0 * NCH + c: m0 * NCH + c + 1]), 1)
                        v.wait_ge(mix.h, mix.n)
                        mix.inc(v.scalar_tensor_tensor(
                            sl(slot, c, 1, TPC - 1),
                            xf[:, c * TPC:(c + 1) * TPC - 1],
                            mu[:, m1 * NCH + c: m1 * NCH + c + 1],
                            sl(slot, c, 1, TPC - 1),
                            op0=ALU.mult, op1=ALU.add), 1)
                        v.wait_ge(mix.h, mix.n)
                        mix.inc(v.tensor_scalar_mul(
                            sl(slot, c, 0, 1), xf[:, c * TPC: c * TPC + 1],
                            mu[:, m0 * NCH + c: m0 * NCH + c + 1]), 1)
                        v.wait_ge(mix.h, mix.n)
                        mix.inc(v.scalar_tensor_tensor(
                            sl(slot, c, 0, 1), bcol[:, c:c + 1],
                            mu[:, m1 * NCH + c: m1 * NCH + c + 1],
                            sl(slot, c, 0, 1),
                            op0=ALU.mult, op1=ALU.add), 1)
                        v.wait_ge(mix.h, mix.n)

            def evR(oc, tt, ps_ap, j):
                @block.scalar
                def _(sc, oc=oc, tt=tt, ps_ap=ps_ap, j=j):
                    sc.wait_ge(ctx.sems["gr_mm"].h, j + 1)
                    ins = sc.activation(sl(2, oc, tt * 512, 512), ps_ap, AF.Sigmoid)
                    ctx.sems["gr_ev"].inc(ins, 1)

            emit_gemm(ctx, block, tag="gr",
                      xb_sb=arena[:, 0:CW], xb_T=TPC,
                      w_dram=wr, w_cols=D, n_in_ch=NCH, n_out_ch=NCH,
                      psum=ps[0:4], evac=evR, tensor_guards=(mix, ini))

            kk_ = ctx.sem("kk")

            def evK(oc, tt, ps_ap, j):
                @block.vector
                def _(v, oc=oc, tt=tt, ps_ap=ps_ap, j=j):
                    v.wait_ge(ctx.sems["gk_mm"].h, j + 1)
                    if j >= 2:
                        v.wait_ge(ctx.sems["gk_ev"].h, j - 1)
                    kk_.inc(v.tensor_scalar_max(
                        tsc[:, (j % 2) * 512:(j % 2) * 512 + 512], ps_ap, 0.0), 1)

                @block.scalar
                def _(sc, oc=oc, tt=tt, j=j):
                    sc.wait_ge(kk_.h, j + 1)
                    ins = sc.activation(sl(0, oc, tt * 512, 512),
                                        tsc[:, (j % 2) * 512:(j % 2) * 512 + 512],
                                        AF.Square)
                    ctx.sems["gk_ev"].inc(ins, 1)

            emit_gemm(ctx, block, tag="gk",
                      xb_sb=arena[:, CW:2 * CW], xb_T=TPC,
                      w_dram=wk, w_cols=D, n_in_ch=NCH, n_out_ch=NCH,
                      psum=ps[4:8], evac=evK,
                      tensor_guards=(ctx.sems["gr_mm"],))

            gv2 = ctx.sem("gv2")

            def evV(oc, tt, ps_ap, j):
                @block.vector
                def _(v, oc=oc, tt=tt, ps_ap=ps_ap, j=j):
                    v.wait_ge(ctx.sems["gv_mm"].h, j + 1)
                    if j >= 2:
                        v.wait_ge(ctx.sems["gv_ev"].h, j - 1)
                    t1 = tsc[:, (j % 2) * 512:(j % 2) * 512 + 512]
                    gv2.inc(v.tensor_mul(t1, ps_ap, sl(2, oc, tt * 512, 512)), 1)
                    v.wait_ge(gv2.h, gv2.n)
                    dst = xf[:, oc * TPC + tt * 512: oc * TPC + tt * 512 + 512]
                    ctx.sems["gv_ev"].inc(v.tensor_add(dst, t1, dst), 1)

            emit_gemm(ctx, block, tag="gv",
                      xb_sb=arena[:, 0:CW], xb_T=TPC,
                      w_dram=wv, w_cols=D, n_in_ch=NCH, n_out_ch=NCH,
                      psum=ps[0:4], evac=evV,
                      tensor_guards=(ctx.sems["gk_ev"], ctx.sems["gr_ev"]))

            sqs = arena[:, CW:2 * CW]
            rows = arena[0:1, 2 * CW:2 * CW + 16384].bitcast(F32)
            app = emit_ln(ctx, block, tag="ln", x_sb=xf, T=TPC, n_ch=NCH,
                          psum4=ps[4:8], onescol32=onescol32,
                          onescol_bf=onescol_bf, onesr32=onesr_sb,
                          sq_bf=sqs, rows=rows, bcast=bcast,
                          eps_ap=epsr[0:1, 0:1],
                          pre_guards=(ctx.sems["gv_ev"],),
                          bank_guards=(ctx.sems["gk_ev"], ini, orl))

            sto = ctx.sem("sto")

            @block.sync
            def _(s_):
                app.wait(s_)
                dst = dap(x6_o, 0, [[TPC, 128], [128 * TPC, NCH], [1, TPC]])
                src = dap(xf, 0, [[NCH * TPC, 128], [TPC, NCH], [1, TPC]])
                sto.inc(s_.dma_start(out=dst, in_=src), 16)
                sto.wait(s_)
    return nc


def run_L2(xTs, inputs, stage, payload=None):
    onesr = np.ones((1, 128), np.float32)
    if stage == "a1":
        nc = build_L2a1()
        bias_v = np.zeros(6 * 4096, np.float32)
        bias_v[0:4096] = np.asarray(inputs["L2_attn_b_h"], np.float32)
        bias_v[4096:4224] = np.asarray(inputs["L2_attn_b_qk"], np.float32)
        bias_v[2 * 4096:2 * 4096 + 1024] = np.asarray(inputs["L2_attn_ln_g"], np.float32)
        bias_v[3 * 4096:3 * 4096 + 1024] = np.asarray(inputs["L2_attn_ln_b"], np.float32)
        in_maps = [{"x_i": xTs[c],
                    "wh": np.asarray(inputs["L2_attn_W_h"], np.float32),
                    "wqk": np.asarray(inputs["L2_attn_W_qk"], np.float32),
                    "bias_i": bias_v, "onesr": onesr} for c in range(NCORE)]
    elif stage == "a2":
        nc = build_L2a2()
        osgb = np.concatenate([np.asarray(inputs["L2_attn_os_gamma"], np.float32).reshape(-1),
                               np.asarray(inputs["L2_attn_os_beta"], np.float32).reshape(-1)])
        masks = _l2_masks()
        re = np.ascontiguousarray(np.asarray(inputs["L2_attn_rel_emb"], np.float32)[:, 0])
        in_maps = [{"v_i": payload[c]["vT"], "qk_i": payload[c]["qkT"],
                    "osgb_i": osgb, "masks_i": masks, "re_i": re,
                    "onesr": onesr, "ident": _I128} for c in range(NCORE)]
    elif stage == "b":
        nc = build_L2b()
        in_maps = []
        for c in range(NCORE):
            b = c // 4
            lkvp = np.stack([payload[cc]["lkvT"] for cc in range(b * 4, b * 4 + 4)])
            in_maps.append({"x_i": xTs[c], "quad_i": payload[c]["quadT"],
                            "gate_i": payload[c]["gateT"],
                            "linq_i": payload[c]["linqT"],
                            "lkvp_i": np.ascontiguousarray(lkvp),
                            "wout": np.asarray(inputs["L2_attn_W_out"], np.float32),
                            "bout": np.asarray(inputs["L2_attn_b_out"], np.float32),
                            "onesr": onesr})
    else:
        nc = build_L2c()
        mus_v = np.zeros(4 * D, np.float32)
        mur = np.asarray(inputs["L2_ffn_mu_r"], np.float32)
        muk = np.asarray(inputs["L2_ffn_mu_k"], np.float32)
        mus_v[0:D] = mur
        mus_v[D:2 * D] = 1.0 - mur
        mus_v[2 * D:3 * D] = muk
        mus_v[3 * D:4 * D] = 1.0 - muk
        in_maps = []
        for c in range(NCORE):
            r = c % 4
            if r == 0:
                bc = np.zeros(D, np.float32)
            else:
                bc = np.ascontiguousarray(payload[c - 1][:, -1])
            in_maps.append({"x_i": xTs[c], "bcol_i": bc,
                            "wr": np.ascontiguousarray(np.asarray(inputs["L2_ffn_W_r"], np.float32).T),
                            "wk": np.ascontiguousarray(np.asarray(inputs["L2_ffn_W_k"], np.float32).T),
                            "wv": np.ascontiguousarray(np.asarray(inputs["L2_ffn_W_v"], np.float32).T),
                            "mus": mus_v, "onesr": onesr})
    res = run_bass_kernel_spmd(nc, in_maps, core_ids=list(range(NCORE)))
    return res.results


def _l2_masks():
    import math as _m
    G = 256
    NUM_BUCKETS = 32
    MAX_DIST = 128
    pos = np.arange(G)
    rel = pos[None, :] - pos[:, None]
    n = -rel
    nb = NUM_BUCKETS // 2
    ret = (n < 0).astype(np.int64) * nb
    n = np.abs(n)
    max_exact = nb // 2
    is_small = n < max_exact
    val_large = max_exact + (np.log(np.maximum(n, 1) / max_exact)
                             / _m.log(MAX_DIST / max_exact) * (nb - max_exact)).astype(np.int64)
    val_large = np.minimum(val_large, nb - 1)
    bucket = ret + np.where(is_small, n, val_large)
    bucketT = bucket.T
    masks = np.zeros((32, 2, 128, 256), dtype=BF)
    for b in range(32):
        m = (bucketT == b).astype(np.float32)
        masks[b, 0] = m[0:128, :]
        masks[b, 1] = m[128:256, :]
    return masks


def build_L3b():
    ctx = LaunchCtx("L3b")
    nc = ctx.nc
    x_i = ctx.din("x_i", [D, TPC])
    wT_i = ctx.din("wT_i", [D, D])
    onesr = ctx.din("onesr", [1, 128])
    ident = ctx.din("ident", [128, 128])
    out_o = ctx.dout("x_out", [TPC, D])

    with ctx.stack:
        xf = ctx.sbuf("xf", [128, NCH * TPC], F32)
        xb = ctx.sbuf("xb", [128, NCH * TPC], BF16)
        wsf = ctx.sbuf("wsf", [128, 1024], F32)
        wsb = ctx.sbuf("wsb", [128, NCH * 1024], BF16)
        sqs = ctx.sbuf("sqs", [128, NCH * TPC], BF16)
        stg = ctx.sbuf("stg", [128, 2 * 1024], F32)
        onescol32 = ctx.sbuf("onescol32", [128, 1], F32)
        onescol_bf = ctx.sbuf("onescol_bf", [128, 1], BF16)
        onesr_sb = ctx.sbuf("onesr_sb", [1, 128], F32)
        idsb = ctx.sbuf("idsb", [128, 128], F32)
        rows = ctx.sbuf("rows", [1, 4 * TPC], F32)
        bcast = ctx.sbuf("bcast", [128, 2 * 512], F32)
        epsr = ctx.sbuf("epsr", [1, 1], F32)
        ps = [ctx.psum("ps%d" % i) for i in range(8)]

        with nc.Block() as block:
            ini = ctx.sem("ini")
            xld = ctx.sem("xld")
            wld = ctx.sem("wld")
            wsg = ctx.sem("wsg")
            cst = ctx.sem("cst")
            gmm = ctx.sem("gmm")
            gev = ctx.sem("gev")
            trm = ctx.sem("trm")
            trv = ctx.sem("trv")
            stp = [ctx.sem("stp0"), ctx.sem("stp1")]

            @block.gpsimd
            def _(g):
                ini.inc(g.memset(onescol32[:], 1.0), 1)
                ini.inc(g.memset(onescol_bf[:], 1.0), 1)
                ini.inc(g.memset(epsr[:], 1e-5), 1)
                src = dap(x_i, 0, [[TPC, 128], [128 * TPC, NCH], [1, TPC]])
                xld.inc(g.dma_start(
                    out=xb[:].rearrange("p (c t) -> p c t", c=NCH), in_=src), 16)

            xfl = ctx.sem("xfl")

            @block.sync
            def _(s_):
                cst.inc(s_.dma_start(out=onesr_sb[:], in_=onesr.ap()), 16)
                cst.inc(s_.dma_start(out=idsb[:], in_=ident.ap()), 16)
                src = dap(x_i, 0, [[TPC, 128], [128 * TPC, NCH], [1, TPC]])
                xfl.inc(s_.dma_start(
                    out=xf[:].rearrange("p (c t) -> p c t", c=NCH), in_=src), 16)
                for ic in range(NCH):
                    if ic >= 1:
                        s_.wait_ge(wsg.h, ic)
                    wld.inc(s_.dma_start(out=wsf[:, :],
                                         in_=wT_i.ap()[ic * 128:(ic + 1) * 128, :]), 16)

            @block.scalar
            def _(sc):
                for ic in range(NCH):
                    sc.wait_ge(wld.h, 16 * (ic + 1))
                    ins = sc.activation(wsb[:, ic * 1024:(ic + 1) * 1024],
                                        wsf[:, :], AF.Sign)
                    wsg.inc(ins, 1)

            @block.tensor
            def _(t):
                xld.wait(t)
                wsg.wait(t)
                for oc in range(NCH):
                    for tt in range(4):
                        j = oc * 4 + tt
                        if j >= 4:
                            t.wait_ge(gev.h, j - 3)
                        for ic in range(NCH):
                            ins = t.matmul(
                                ps[j % 4][:, 0:512],
                                wsb[:, ic * 1024 + oc * 128: ic * 1024 + oc * 128 + 128],
                                xb[:, ic * TPC + tt * 512: ic * TPC + tt * 512 + 512],
                                start=(ic == 0), stop=(ic == NCH - 1))
                        gmm.inc(ins, 1)

            @block.vector
            def _(v):
                for oc in range(NCH):
                    for tt in range(4):
                        j = oc * 4 + tt
                        v.wait_ge(gmm.h, j + 1)
                        if j == 0:
                            xfl.wait(v)
                        dst = xf[:, oc * TPC + tt * 512: oc * TPC + tt * 512 + 512]
                        ins = v.tensor_add(dst, ps[j % 4][:, 0:512], dst)
                        gev.inc(ins, 1)

            app = emit_ln(ctx, block, tag="ln", x_sb=xf, T=TPC, n_ch=NCH,
                          psum4=ps[4:8], onescol32=onescol32,
                          onescol_bf=onescol_bf, onesr32=onesr_sb,
                          sq_bf=sqs, rows=rows, bcast=bcast,
                          eps_ap=epsr[0:1, 0:1],
                          pre_guards=(gev,), bank_guards=(ini, cst))

            @block.tensor
            def _(t):
                app.wait(t)
                for tt in range(16):
                    for c in range(NCH):
                        bank = 2 * (tt % 2) + (c // 4)
                        w = 2 * tt + (c // 4) - 3
                        if w > 0:
                            t.wait_ge(trv.h, w)
                        ins = t.transpose(
                            ps[bank][:, (c % 4) * 128:(c % 4) * 128 + 128],
                            xf[:, c * TPC + tt * 128: c * TPC + tt * 128 + 128],
                            idsb[:])
                        trm.inc(ins, 1)

            @block.vector
            def _(v):
                for tt in range(16):
                    for half in range(2):
                        v.wait_ge(trm.h, tt * NCH + half * 4 + 4)
                        if tt >= 2:
                            v.wait_ge(stp[tt % 2].h, 16 * (tt // 2))
                        bank = 2 * (tt % 2) + half
                        ins = v.tensor_copy(
                            stg[:, (tt % 2) * 1024 + half * 512:
                                (tt % 2) * 1024 + half * 512 + 512],
                            ps[bank][:, 0:512])
                        trv.inc(ins, 1)

            @block.sync
            def _(s_):
                for tt in range(16):
                    s_.wait_ge(trv.h, 2 * (tt + 1))
                    stp[tt % 2].inc(s_.dma_start(
                        out=out_o.ap()[tt * 128:(tt + 1) * 128, :],
                        in_=stg[:, (tt % 2) * 1024:(tt % 2) * 1024 + 1024]), 16)
                stp[0].wait(s_)
                stp[1].wait(s_)
    return nc


def run_L3b(xTs, inputs):
    nc = build_L3b()
    onesr = np.ones((1, 128), np.float32)
    wT = np.ascontiguousarray(np.asarray(inputs["L3_ffn_weight"], np.float32).T)
    in_maps = [{"x_i": xTs[c], "wT_i": wT, "onesr": onesr, "ident": _I128}
               for c in range(NCORE)]
    res = run_bass_kernel_spmd(nc, in_maps, core_ids=list(range(NCORE)))
    return res.results


def kernel(**inputs):
    inputs = {k: np.asarray(v) for k, v in inputs.items()}

    res = run_L0a(inputs)
    xTs = [res[c]["xT"] for c in range(NCORE)]
    qTs = [res[c]["qT"] for c in range(NCORE)]
    nqs = [res[c]["nrm"][:TPC] for c in range(NCORE)]
    kT_fulls, v_fulls, nk_fulls = [], [], []
    for b in range(2):
        cores = range(b * 4, b * 4 + 4)
        kT_fulls.append(np.ascontiguousarray(
            np.concatenate([res[c]["kT"] for c in cores], axis=1)))
        v_fulls.append(np.ascontiguousarray(
            np.concatenate([res[c]["vT"].T for c in cores], axis=0)))
        nk_fulls.append(np.ascontiguousarray(
            np.concatenate([res[c]["nrm"][TPC:] for c in cores])))
    res = run_L0b(qTs, kT_fulls, v_fulls, nqs, nk_fulls)
    aTs = [res[c]["aT"] for c in range(NCORE)]
    res = run_L0c(aTs, xTs, inputs)
    x1 = [res[c]["x1T"] for c in range(NCORE)]
    res = run_L0de(x1, inputs)
    x2 = [res[c]["x2T"] for c in range(NCORE)]

    res = run_L1a(x2, inputs)
    qTs = [res[c]["qT"] for c in range(NCORE)]
    nqs = [res[c]["nrm"][:TPC] for c in range(NCORE)]
    kT_fulls, v_fulls, nk_fulls = [], [], []
    for b in range(2):
        cores = range(b * 4, b * 4 + 4)
        kT_fulls.append(np.ascontiguousarray(
            np.concatenate([res[c]["kT"] for c in cores], axis=1)))
        v_fulls.append(np.ascontiguousarray(
            np.concatenate([res[c]["vT"].T for c in cores], axis=0)))
        nk_fulls.append(np.ascontiguousarray(
            np.concatenate([res[c]["nrm"][TPC:] for c in cores])))
    res = run_L1b(qTs, kT_fulls, v_fulls, nqs, nk_fulls, inputs)
    aTs = [res[c]["aT"] for c in range(NCORE)]
    res = run_L1c(aTs, x2, inputs)
    x3 = [res[c]["x3T"] for c in range(NCORE)]
    res = run_L1d(x3, inputs)
    x4 = [res[c]["x4T"] for c in range(NCORE)]

    res = run_L2(x4, inputs, "a1")
    pay = [{"vT": res[c]["vT"], "gateT": res[c]["gateT"], "qkT": res[c]["qkT"]}
           for c in range(NCORE)]
    res = run_L2(x4, inputs, "a2", pay)
    for c in range(NCORE):
        pay[c].update(quadT=res[c]["quadT"], linqT=res[c]["linqT"],
                      lkvT=res[c]["lkvT"])
    res = run_L2(x4, inputs, "b", pay)
    x5 = [res[c]["x5T"] for c in range(NCORE)]
    res = run_L2(x5, inputs, "c", x5)
    x6 = [res[c]["x6T"] for c in range(NCORE)]

    res = run_L3a(x6, inputs)
    x7 = [res[c]["x7T"] for c in range(NCORE)]
    res = run_L3b(x7, inputs)

    out = np.empty((B, S, D), np.float32)
    for c in range(NCORE):
        b, r = c // 4, c % 4
        out[b, r * TPC:(r + 1) * TPC] = res[c]["x_out"]
    return out


def build_L3a1():
    ctx = LaunchCtx("L3a1")
    nc = ctx.nc
    x_i = ctx.din("x_i", [D, TPC])
    wat = ctx.din("wat", [D, 3072])
    cs_i = ctx.din("cs_i", [2, 128, TPC], BF16)
    prt_i = ctx.din("prt_i", [128, 128], BF16)
    ident = ctx.din("ident", [128, 128])
    qk_o = ctx.dout("qkT", [2048, TPC])
    vt_o = ctx.dout("vt", [TPC, D])

    with ctx.stack:
        xb = ctx.sbuf("xb", [128, NCH * TPC], BF16)
        qkvb = ctx.sbuf("qkvb", [128, 24 * TPC], BF16)
        cosf = ctx.sbuf("cosf", [128, TPC], BF16)
        sinf = ctx.sbuf("sinf", [128, TPC], BF16)
        prt = ctx.sbuf("prt", [128, 128], BF16)
        idbf = ctx.sbuf("idbf", [128, 128], BF16)
        rsc = ctx.sbuf("rsc", [128, 2 * 512], BF16)
        vstg = ctx.sbuf("vstg", [128, 2 * 1024], BF16)
        ps = [ctx.psum("ps%d" % i) for i in range(6)]
        psb = [ctx.psum("psb%d" % i, (128, 1024), BF16) for i in range(2)]

        with nc.Block() as block:
            cst = ctx.sem("cst")
            xld = ctx.sem("xld")
            rmm = ctx.sem("rmm")
            rev = ctx.sem("rev")
            tr = ctx.sem("tr")
            tev = ctx.sem("tev")
            qso = ctx.sem("qso")
            vso = [ctx.sem("vso0"), ctx.sem("vso1")]

            @block.gpsimd
            def _(g):
                src = dap(x_i, 0, [[TPC, 128], [128 * TPC, NCH], [1, TPC]])
                xld.inc(g.dma_start(
                    out=xb[:].rearrange("p (c t) -> p c t", c=NCH), in_=src), 16)
                xld.inc(g.dma_start(out=idbf[:], in_=ident.ap()), 16)

            @block.sync
            def _(s_):
                cst.inc(s_.dma_start(out=prt[:], in_=prt_i.ap()), 16)
                cst.inc(s_.dma_start(out=cosf[:], in_=cs_i.ap()[0]), 16)
                cst.inc(s_.dma_start(out=sinf[:], in_=cs_i.ap()[1]), 16)

            def evQKV(oc, tt, ps_ap, j):
                @block.scalar
                def _(sc, oc=oc, tt=tt, ps_ap=ps_ap, j=j):
                    sc.wait_ge(ctx.sems["gqkv_mm"].h, j + 1)
                    ins = sc.activation(
                        qkvb[:, oc * TPC + tt * 512: oc * TPC + tt * 512 + 512],
                        ps_ap, AF.Copy)
                    ctx.sems["gqkv_ev"].inc(ins, 1)

            emit_gemm(ctx, block, tag="gqkv", xb_sb=xb, xb_T=TPC,
                      w_dram=wat, w_cols=3072, n_in_ch=NCH, n_out_ch=24,
                      psum=ps[0:4], evac=evQKV, tensor_guards=(xld,))

            @block.tensor
            def _(t):
                cst.wait(t)
                ctx.sems["gqkv_ev"].wait(t)
                for c in range(16):
                    for tt in range(4):
                        j = c * 4 + tt
                        if j >= 4:
                            t.wait_ge(rev.h, 3 * (j - 3))
                        ins = t.matmul(
                            ps[j % 4][:, 0:512], prt[:, :],
                            qkvb[:, c * TPC + tt * 512: c * TPC + tt * 512 + 512],
                            start=True, stop=True)
                        rmm.inc(ins, 1)

            @block.vector
            def _(v):
                for c in range(16):
                    for tt in range(4):
                        j = c * 4 + tt
                        v.wait_ge(rmm.h, j + 1)
                        sl_q = qkvb[:, c * TPC + tt * 512: c * TPC + tt * 512 + 512]
                        t1 = rsc[:, (j % 2) * 512:(j % 2) * 512 + 512]
                        cseg = cosf[:, tt * 512:(tt + 1) * 512]
                        sseg = sinf[:, tt * 512:(tt + 1) * 512]
                        rev.inc(v.tensor_mul(t1, sl_q, cseg), 1)
                        v.wait_ge(rev.h, rev.n)
                        rev.inc(v.tensor_mul(sl_q, ps[j % 4][:, 0:512], sseg), 1)
                        v.wait_ge(rev.h, rev.n)
                        rev.inc(v.tensor_add(sl_q, sl_q, t1), 1)
                        v.wait_ge(rev.h, rev.n)

            @block.gpsimd
            def _(g):
                rev.wait(g)
                src = dap(qkvb, 0, [[24 * TPC, 128], [TPC, 16], [1, TPC]])
                dst = dap(qk_o, 0, [[TPC, 128], [128 * TPC, 16], [1, TPC]])
                qso.inc(g.dma_start(out=dst, in_=src), 16)

            @block.tensor
            def _(t):
                ctx.sems["gqkv_ev"].wait(t)
                xld.wait(t)
                for j in range(128):
                    tcc, ec = j // 8, j % 8
                    if j >= 2:
                        t.wait_ge(tev.h, j - 1)
                    src = qkvb[:, (16 + ec) * TPC + tcc * 128:
                               (16 + ec) * TPC + tcc * 128 + 128]
                    ins = t.transpose(psb[j % 2][:, 0:128], src, idbf[:])
                    tr.inc(ins, 1)

            @block.vector
            def _(v):
                for j in range(128):
                    tcc, ec = j // 8, j % 8
                    v.wait_ge(tr.h, j + 1)
                    if ec == 0 and tcc >= 2:
                        v.wait_ge(vso[tcc % 2].h, 16 * (tcc // 2))
                    ins = v.tensor_copy(
                        vstg[:, (tcc % 2) * 1024 + ec * 128:
                             (tcc % 2) * 1024 + ec * 128 + 128],
                        psb[j % 2][:, 0:128])
                    tev.inc(ins, 1)

            @block.gpsimd
            def _(g):
                for tcc in range(16):
                    g.wait_ge(tev.h, 8 * (tcc + 1))
                    vso[tcc % 2].inc(g.dma_start(
                        out=vt_o.ap()[tcc * 128:(tcc + 1) * 128, :],
                        in_=vstg[:, (tcc % 2) * 1024:(tcc % 2) * 1024 + 1024]), 16)

            @block.sync
            def _(s_):
                qso.wait(s_)
                vso[0].wait(s_)
                vso[1].wait(s_)
    return nc



def build_L3a2():
    ctx = LaunchCtx("L3a2")
    nc = ctx.nc
    x_i = ctx.din("x_i", [D, TPC])
    qk_i = ctx.din("qk_i", [2048, TPC])
    vt_i = ctx.din("vt_i", [TPC, D])
    wpr = ctx.din("wpr", [D, D])
    onesr = ctx.din("onesr", [1, 128])
    x7_o = ctx.dout("x7T", [D, TPC])

    NTC = 32
    with ctx.stack:
        qh = ctx.sbuf("qh", [64, 2 * TPC], BF16)
        kh = ctx.sbuf("kh", [64, 2 * TPC], BF16)
        vh = ctx.sbuf("vh", [64, NTC * 64], BF16)
        Eh = ctx.sbuf("Eh", [64, NTC * 64], BF16)
        ybT = ctx.sbuf("ybT", [128, NCH * TPC], BF16)
        rrow = ctx.sbuf("rrow", [1, 4 * 512], F32)
        rb3 = ctx.sbuf("rb3", [64, 4 * 512], F32)
        zq = ctx.sbuf("zq", [128, NCH * TPC], F32)
        xq = ctx.sbuf("xq", [128, 2 * 512], F32)
        onescol_bf = ctx.sbuf("onescol_bf", [128, 1], BF16)
        onescol32 = ctx.sbuf("onescol32", [128, 1], F32)
        onesr_sb = ctx.sbuf("onesr_sb", [1, 128], F32)
        rows = ctx.sbuf("rows", [1, 4 * TPC], F32)
        bcast = ctx.sbuf("bcast", [128, 2 * 512], F32)
        epsr = ctx.sbuf("epsr", [1, 1], F32)
        ps = [ctx.psum("ps%d" % i) for i in range(8)]

        with nc.Block() as block:
            ini = ctx.sem("ini")
            cst = ctx.sem("cst")
            qkl = [ctx.sem("qkl0"), ctx.sem("qkl1")]
            smm = ctx.sem("smm")
            sex = ctx.sem("sex")
            ssum = ctx.sem("ssum")
            rcp = ctx.sem("rcp")
            pvm = ctx.sem("pvm")
            yev = ctx.sem("yev")
            sto = ctx.sem("sto")

            @block.gpsimd
            def _(g):
                ini.inc(g.memset(onescol_bf[:], 1.0), 1)
                ini.inc(g.memset(onescol32[:], 1.0), 1)
                ini.inc(g.memset(epsr[:], 1e-5), 1)
                for h in range(16):
                    if h >= 2:
                        g.wait_ge(smm.h, NTC * (h - 1))
                    b = (h % 2)
                    src = dap(qk_i, h * 64 * TPC, [[TPC, 64], [1, TPC]])
                    qkl[b].inc(g.dma_start(out=qh[0:64, b * TPC:(b + 1) * TPC],
                                           in_=src), 16)
                    src = dap(qk_i, (1024 + h * 64) * TPC, [[TPC, 64], [1, TPC]])
                    qkl[b].inc(g.dma_start(out=kh[0:64, b * TPC:(b + 1) * TPC],
                                           in_=src), 16)
                    if h >= 1:
                        g.wait_ge(pvm.h, NTC * h)
                    src = dap(vt_i, h * 64, [[D, 64], [64 * D, NTC], [1, 64]])
                    qkl[b].inc(g.dma_start(
                        out=dap(vh, 0, [[NTC * 64, 64], [64, NTC], [1, 64]]),
                        in_=src), 16)

            @block.sync
            def _(s_):
                cst.inc(s_.dma_start(out=onesr_sb[:], in_=onesr.ap()), 16)

            for h in range(16):
                b = h % 2

                @block.tensor
                def _(t, h=h, b=b):
                    t.wait_ge(qkl[b].h, 48 * (h // 2 + 1))
                    if h == 0:
                        t.wait_ge(ini.h, 3)
                        cst.wait(t)
                    if h >= 1:
                        t.wait_ge(yev.h, 4 * h)
                    for tc in range(NTC):
                        ins = t.matmul(
                            ps[tc // 8][0:64, (tc % 8) * 64:(tc % 8) * 64 + 64],
                            kh[0:64, b * TPC + tc * 64: b * TPC + tc * 64 + 64],
                            qh[0:64, b * TPC + tc * 64: b * TPC + tc * 64 + 64],
                            start=True, stop=True)
                        smm.inc(ins, 1)

                @block.scalar
                def _(sc, h=h):
                    if h >= 1:
                        sc.wait_ge(pvm.h, NTC * h)
                    for bk in range(4):
                        sc.wait_ge(smm.h, NTC * h + 8 * (bk + 1))
                        ins = sc.activation(Eh[0:64, bk * 512:(bk + 1) * 512],
                                            ps[bk][0:64, 0:512], AF.Exp,
                                            scale=0.125)
                        sex.inc(ins, 1)

                @block.tensor
                def _(t, h=h):
                    for bk in range(4):
                        t.wait_ge(sex.h, 4 * h + bk + 1)
                        if h >= 1 or bk >= 1:
                            t.wait_ge(rcp.h, 8 * h + 2 * bk - 1)
                        ins = t.matmul(ps[4][0:1, 0:512],
                                       onescol_bf[0:64, 0:1],
                                       Eh[0:64, bk * 512:(bk + 1) * 512],
                                       start=True, stop=True)
                        ssum.inc(ins, 1)
                        t.wait_ge(rcp.h, 8 * h + 2 * bk + 1)
                        ins = t.matmul(ps[5][0:64, 0:512], onesr_sb[0:1, 0:64],
                                       rrow[0:1, bk * 512:(bk + 1) * 512],
                                       start=True, stop=True)
                        ssum.inc(ins, 1)

                @block.tensor
                def _(t, h=h, b=b):
                    for tc in range(NTC):
                        ins = t.matmul(
                            ps[tc // 8][0:64, (tc % 8) * 64:(tc % 8) * 64 + 64],
                            vh[0:64, tc * 64: tc * 64 + 64],
                            Eh[0:64, tc * 64:(tc + 1) * 64],
                            start=True, stop=True)
                        pvm.inc(ins, 1)

                @block.vector
                def _(v, h=h):
                    for bk in range(4):
                        v.wait_ge(ssum.h, 8 * h + 2 * bk + 1)
                        ins = v.reciprocal(rrow[0:1, bk * 512:(bk + 1) * 512],
                                           ps[4][0:1, 0:512])
                        rcp.inc(ins, 1)
                        v.wait_ge(rcp.h, rcp.n)
                        v.wait_ge(ssum.h, 8 * h + 2 * bk + 2)
                        ins = v.tensor_copy(rb3[0:64, bk * 512:(bk + 1) * 512],
                                            ps[5][0:64, 0:512])
                        rcp.inc(ins, 1)
                        v.wait_ge(rcp.h, rcp.n)

                @block.vector
                def _(v, h=h):
                    for bk in range(4):
                        v.wait_ge(pvm.h, NTC * h + 8 * (bk + 1))
                        dst = dap(ybT, (h % 2) * 64 * (NCH * TPC)
                                  + (h // 2) * TPC + bk * 512,
                                  [[NCH * TPC, 64], [1, 512]])
                        ins = v.tensor_mul(dst, ps[bk][0:64, 0:512],
                                           rb3[0:64, bk * 512:(bk + 1) * 512])
                        yev.inc(ins, 1)
                        v.wait_ge(yev.h, yev.n)

            xqlp = [ctx.sem("xql0"), ctx.sem("xql1")]

            def evP(oc, tt, ps_ap, j):
                @block.sync
                def _(s_, oc=oc, tt=tt, j=j):
                    if j >= 2:
                        s_.wait_ge(ctx.sems["gp_ev"].h, j - 1)
                    src = dap(x_i, oc * 128 * TPC + tt * 512, [[TPC, 128], [1, 512]])
                    xqlp[j % 2].inc(s_.dma_start(
                        out=xq[:, (j % 2) * 512:(j % 2) * 512 + 512], in_=src), 16)

                @block.vector
                def _(v, oc=oc, tt=tt, ps_ap=ps_ap, j=j):
                    v.wait_ge(ctx.sems["gp_mm"].h, j + 1)
                    v.wait_ge(xqlp[j % 2].h, 16 * (j // 2 + 1))
                    dst = zq[:, oc * TPC + tt * 512: oc * TPC + tt * 512 + 512]
                    ins = v.tensor_add(dst, ps_ap,
                                       xq[:, (j % 2) * 512:(j % 2) * 512 + 512])
                    ctx.sems["gp_ev"].inc(ins, 1)

            emit_gemm(ctx, block, tag="gp", xb_sb=ybT, xb_T=TPC,
                      w_dram=wpr, w_cols=D, n_in_ch=NCH, n_out_ch=NCH,
                      psum=ps[0:4], evac=evP,
                      tensor_guards=(yev,))

            app = emit_ln(ctx, block, tag="ln", x_sb=zq, T=TPC, n_ch=NCH,
                          psum4=(ps[4], ps[5], ps[6], ps[7]),
                          onescol32=onescol32, onescol_bf=onescol_bf,
                          onesr32=onesr_sb, sq_bf=ybT, rows=rows, bcast=bcast,
                          eps_ap=epsr[0:1, 0:1],
                          pre_guards=(ctx.sems["gp_ev"],),
                          bank_guards=(ctx.sems["gp_ev"], ini, cst, rcp))

            @block.sync
            def _(s_):
                app.wait(s_)
                dst = dap(x7_o, 0, [[TPC, 128], [128 * TPC, NCH], [1, TPC]])
                src = zq[:].rearrange("p (c t) -> p c t", c=NCH)
                sto.inc(s_.dma_start(out=dst, in_=src), 16)
                sto.wait(s_)
    return nc


def run_L3a(xTs, inputs):
    onesr = np.ones((1, 128), np.float32)
    inv_freq = 1.0 / (10000.0 ** (np.arange(0, 64, 2, dtype=np.float64) / 64))
    tpos = np.arange(TPC) % 64
    cosf = np.zeros((128, TPC), np.float32)
    sinf = np.zeros((128, TPC), np.float32)
    for p in range(128):
        f = inv_freq[p % 32]
        cosf[p] = np.cos(tpos * f)
        sinf[p] = np.sin(tpos * f)
    cs = np.stack([cosf, sinf]).astype(BF)
    PR = np.zeros((128, 128), np.float32)
    for f in range(128):
        if f % 64 < 32:
            PR[f, f + 32] = 1.0
        else:
            PR[f, f - 32] = -1.0
    prt = np.ascontiguousarray(PR.T).astype(BF)
    wat = np.ascontiguousarray(np.asarray(inputs["L3_attn_w_attn"], np.float32).T)
    wpr = np.ascontiguousarray(np.asarray(inputs["L3_attn_w_proj"], np.float32).T)
    nc1 = build_L3a1()
    in_maps = [{"x_i": xTs[c], "wat": wat, "cs_i": cs, "prt_i": prt,
                "ident": _I128} for c in range(NCORE)]
    r1 = run_bass_kernel_spmd(nc1, in_maps, core_ids=list(range(NCORE)))
    nc2 = build_L3a2()
    in_maps = [{"x_i": xTs[c], "qk_i": r1.results[c]["qkT"],
                "vt_i": r1.results[c]["vt"], "wpr": wpr, "onesr": onesr}
               for c in range(NCORE)]
    r2 = run_bass_kernel_spmd(nc2, in_maps, core_ids=list(range(NCORE)))
    return r2.results
